# version 12
# baseline (speedup 1.0000x reference)
import contextlib
import numpy as np
import concourse.bass as bass
import concourse.mybir as mybir
from concourse.bass_utils import run_bass_kernel_spmd

F32 = mybir.dt.float32
BF16 = mybir.dt.bfloat16
AF = mybir.ActivationFunctionType
ALU = mybir.AluOpType
AX = mybir.AxisListType

NCORES = 8
T = 2048
D = 1024
DC = 8
TG = 4
NTG = 512
NEXP_LOC = 8
EPS = 1e-5

PHASES = 99
DEBUG_OUT = None

ENGS = ("sync", "scalar", "vector", "tensor", "gpsimd")


class KB:
    def __init__(self):
        self.nc = bass.Bass("TRN2", target_bir_lowering=False)
        self.stack = contextlib.ExitStack()
        self.q = {e: [] for e in ENGS}
        self.sems = {}
        self.ct = {}
        self.deltas = {}
        self.waited = {e: {} for e in ENGS}

    def sem(self, name):
        if name not in self.sems:
            self.sems[name] = self.stack.enter_context(self.nc.semaphore(name))
            self.ct[name] = 0
        return self.sems[name]

    def bump(self, name, k=1):
        self.sem(name)
        self.ct[name] += k
        ev = (name, self.ct[name])
        self.deltas[ev] = k
        return ev

    def now(self, name):
        self.sem(name)
        return (name, self.ct[name])

    def emit(self, eng, build, waits=(), inc=None):
        w = []
        for ev in waits:
            if ev is None:
                continue
            s, v = ev
            if v <= 0 or self.waited[eng].get(s, -1) >= v:
                continue
            self.waited[eng][s] = v
            w.append((self.sems[s], v))
        if inc is not None:
            isem = self.sems[inc[0]]
            idelta = self.deltas.pop(inc)
        else:
            isem = idelta = None

        def fn(e, build=build, w=tuple(w), isem=isem, idelta=idelta):
            for s, v in w:
                e.wait_ge(s, v)
            ins = build(e)
            if isem is not None:
                ins.then_inc(isem, idelta)
        self.q[eng].append(fn)

    _dma_rot = {}

    def dma(self, eng, out, in_, waits=()):
        base = {"sync": "dS", "gpsimd": "dG", "scalar": "dC"}[eng]
        r = self._dma_rot.get(eng, 0)
        self._dma_rot[eng] = r + 1
        qsem = f"{base}{r % 3}"
        prev = self.now(qsem)
        ev = self.bump(qsem, 16)

        def build(e, out=out, in_=in_):
            return e.dma_start(out=out, in_=in_)
        self.emit(eng, build, waits=list(waits) + [prev], inc=ev)
        return ev

    def qdone(self, eng):
        base = {"sync": "dS", "gpsimd": "dG", "scalar": "dC"}[eng]
        return [self.now(f"{base}{i}") for i in range(3)]


def build_program():
    kb = KB()
    nc = kb.nc
    st = kb.stack

    def din(name, shape, dtype=F32):
        return nc.dram_tensor(name, list(shape), dtype, kind="ExternalInput").ap()

    x_in = din("x", (D, T))
    y_out = nc.dram_tensor("y", [D, T], F32, kind="ExternalOutput").ap()

    wa = [din(f"wa{p}", (8, 128, 1024)) for p in range(4)]
    wao = din("wao", (8, 128, 1024))
    a_log = din("a_log", (128, 8))

    phi = din("phi", (8, 128, 2048))
    phib = din("phib", (1, 2048))
    expw = din("expw", (8, 128, 8192))
    expb = din("expb", (8, 1024))

    wbin = din("wbin", (16, 128, 1024))
    kx = din("kx", (128, 32))
    kz = din("kz", (128, 32))
    wxp = din("wxp", (128, 768))
    wdt = din("wdt", (64, 1024))
    b_dt = din("b_dt", (128, 8))
    a_logb = din("a_logb", (128, 128))
    d_par = din("d_par", (128, 8))
    b_outb = din("b_outb", (128, 8))
    wbo = din("wbo", (8, 128, 2048))

    identm = din("identm", (128, 128))
    nidentm = din("nidentm", (128, 128))
    ones1 = din("ones1", (1, 128))
    onescol = din("onescol", (128, 1))
    selb = din("selb", (128, 2048))
    selc = din("selc", (128, 2048))
    sel8 = din("sel8", (8, 256))

    rs_in = nc.dram_tensor("rs_in", [2048, 1024], F32).ap()
    rs_out = nc.dram_tensor("rs_out", [256, 1024], F32).ap()
    ag_in = nc.dram_tensor("ag_in", [256, 1024], BF16).ap()
    ag_out = nc.dram_tensor("ag_out", [2048, 1024], BF16, addr_space="Shared").ap()
    xspill = nc.dram_tensor("xspill", [128, DC * T], F32).ap()
    mdspill = nc.dram_tensor("mdspill", [128, DC * T], F32).ap()
    statd = nc.dram_tensor("statd", [2, 2048], F32).ap()
    statd2 = nc.dram_tensor("statd2", [2, 2048], F32).ap()
    wexp_bf = nc.dram_tensor("wexp_bf", [8, 128, 8192], BF16).ap()
    phi_bf = nc.dram_tensor("phi_bf", [8, 128, 2048], BF16).ap()

    probe = st.enter_context(nc.sbuf_tensor("probe", [128, 32], mybir.dt.uint8))
    BASE = (nc.lookup_mloc(probe).addr + 32 + 2048 + 31) // 32 * 32

    def alloc(name, shape, dtype, offset):
        return nc.alloc_sbuf_tensor_at(name, shape, dtype, offset=BASE + offset)

    XRES = alloc("XRES", [128, DC, T], F32, offset=0)
    LNX = alloc("LNX", [128, DC, T], BF16, offset=65536)
    CON = 195072

    co = [CON]

    def cbuf(name, free_shape, dtype):
        h = alloc(name, [128] + list(free_shape), dtype, offset=co[0])
        per = int(np.prod(free_shape)) * mybir.dt.size(dtype)
        co[0] += (per + 31) // 32 * 32
        return h

    IDENT = cbuf("IDENT", [128], BF16)
    NIDENT = cbuf("NIDENT", [128], BF16)
    ONESC = cbuf("ONESC", [1], BF16)
    ABASE = cbuf("ABASE", [8], F32)
    ALOGT = cbuf("ALOGT", [8], F32)
    APOS = cbuf("APOS", [128], F32)
    NBDT = cbuf("NBDT", [8], F32)
    BDTT = cbuf("BDTT", [8], F32)
    DPAR = cbuf("DPAR", [8], F32)
    BOUTB = cbuf("BOUTB", [8], F32)
    KXT = cbuf("KXT", [32], F32)
    KZT = cbuf("KZT", [32], F32)
    EPSC = cbuf("EPSC", [1], F32)
    SELBB = cbuf("SELBB", [2048], BF16)
    SELCB = cbuf("SELCB", [2048], BF16)
    assert BASE + co[0] <= 229376, (BASE, co[0])

    SCR = 98304
    ONES1 = alloc("ONES1", [1, 128], F32, offset=SCR)
    PHIB = alloc("PHIB", [1, 2048], BF16, offset=SCR + 512)
    WDT = alloc("WDT", [64, 1024], BF16, offset=SCR + 4608)
    EXPB = alloc("EXPB", [8, 1024], BF16, offset=SCR + 6656)
    WXPB = alloc("WXPB", [128, 768], BF16, offset=SCR + 8704)
    SEL8 = alloc("SEL8", [8, 256], BF16, offset=SCR + 10240)
    SMALL = SCR + 10752

    pbanks = [st.enter_context(nc.psum_tensor(f"pb{i}", [128, 512], F32))
              for i in range(7)]
    ptr = st.enter_context(nc.psum_tensor("ptr", [128, 1024], BF16))

    class Pool:
        def __init__(self, banks):
            self.b = banks
            self.ev = [None] * len(banks)
            self.i = 0

        def get(self):
            j = self.i % len(self.b)
            self.i += 1
            return j, self.b[j], self.ev[j]

        def done(self, j, ev):
            self.ev[j] = ev

    pool = Pool(pbanks)

    def MM(bank_ap, lhsT, rhs, start, stop, waits=(), inc=None):
        def b(e, a=bank_ap, l=lhsT, r=rhs, s1=start, s2=stop):
            return e.matmul(a, l, r, start=s1, stop=s2)
        kb.emit("tensor", b, waits=waits, inc=inc)

    def ACT(out, in_, func, scale=None, bias=None, accum=None, waits=(), inc=None):
        def b(e, out=out, in_=in_, func=func, scale=scale, bias=bias, accum=accum):
            kw = {}
            if scale is not None:
                kw["scale"] = scale
            if bias is not None:
                kw["bias"] = bias
            if accum is not None:
                kw["accum_out"] = accum
            return e.activation(out=out, in_=in_, func=func, **kw)
        kb.emit("scalar", b, waits=waits, inc=inc)

    def V(meth, *args, waits=(), inc=None, **kwargs):
        def b(e, meth=meth, args=args, kwargs=kwargs):
            return getattr(e, meth)(*args, **kwargs)
        kb.emit("vector", b, waits=waits, inc=inc)

    def G(meth, *args, waits=(), inc=None, **kwargs):
        def b(e, meth=meth, args=args, kwargs=kwargs):
            return getattr(e, meth)(*args, **kwargs)
        kb.emit("gpsimd", b, waits=waits, inc=inc)

    def tg(g):
        return slice(g * NTG, (g + 1) * NTG)

    ev_x = kb.dma("sync", XRES[:],
                  x_in.rearrange("(k p) t -> p k t", p=128))
    e_alog = kb.dma("sync", ALOGT[:], a_log)
    e_bdt = kb.dma("sync", BDTT[:], b_dt)
    e_apos = kb.dma("sync", APOS[:], a_logb)
    kb.dma("sync", DPAR[:], d_par)
    kb.dma("sync", BOUTB[:], b_outb)
    kb.dma("sync", KXT[:], kx)
    kb.dma("sync", KZT[:], kz)
    ev_ones1 = kb.dma("sync", ONES1[:], ones1)

    kb.dma("gpsimd", IDENT[:], identm)
    kb.dma("gpsimd", NIDENT[:], nidentm)
    ev_onesc = kb.dma("gpsimd", ONESC[:], onescol)
    kb.dma("gpsimd", SELBB[:], selb)
    kb.dma("gpsimd", SELCB[:], selc)
    ev_sel8 = kb.dma("gpsimd", SEL8[:], sel8)
    kb.dma("gpsimd", PHIB[:], phib)
    kb.dma("gpsimd", WDT[:], wdt)
    kb.dma("gpsimd", EXPB[:], expb)
    kb.dma("gpsimd", WXPB[:], wxp)

    ev_ab = kb.bump("A")
    ACT(ABASE[:], ALOGT[:], AF.Sigmoid, waits=[e_alog], inc=ev_ab)
    ev_aposx = kb.bump("A")
    ACT(APOS[:], APOS[:], AF.Exp, waits=[e_apos], inc=ev_aposx)
    ev_nbdt = kb.bump("V")
    V("tensor_scalar_mul", NBDT[:], BDTT[:], -1.0, waits=[e_bdt], inc=ev_nbdt)
    ev_eps = kb.bump("V")
    V("memset", EPSC[:], EPS, inc=ev_eps)

    LNS = SMALL
    XB = alloc("XB", [128, DC, T], BF16, offset=LNS + 4096)
    STG0 = alloc("STG0", [1, T], F32, offset=LNS + 36864)
    STG1 = alloc("STG1", [1, T], F32, offset=LNS + 45056)
    STW0 = alloc("STW0", [128, 16], F32, offset=LNS + 53248)
    STW1 = alloc("STW1", [128, 16], F32, offset=LNS + 53312)
    MUR = alloc("MUR", [128, T], F32, offset=LNS + 53376)
    IVR = alloc("IVR", [128, T], BF16, offset=LNS + 61568)
    LTMP = alloc("LTMP", [128, T], BF16, offset=LNS + 69760)
    LVT = alloc("LVT", [128, 16], F32, offset=LNS + 77952)
    SQ = alloc("SQ", [128, T], BF16, offset=65536)
    SQ2 = alloc("SQ2", [128, T], BF16, offset=69632)
    LN_END = LNS + 78016

    def layernorm(x_ev, guard=None):
        x_evs = x_ev if isinstance(x_ev, list) else [x_ev] * DC
        korder = list(range(DC))
        ev_xb = [None] * DC
        for k in korder:
            ev_xb[k] = kb.bump("A")
            ACT(XB[:, k, :], XRES[:, k, :], AF.Copy,
                waits=[x_evs[k], guard], inc=ev_xb[k])
        sx = []
        for g in range(TG):
            j, bank, free = pool.get()
            for i, k in enumerate(korder):
                w = [ev_xb[k], ev_onesc] + ([free] if i == 0 else [])
                MM(bank[0:1, :], ONESC[:], XB[:, k, tg(g)],
                   start=(i == 0), stop=(i == DC - 1),
                   waits=w, inc=(kb.bump("T") if i == DC - 1 else None))
            sx.append((j, bank, kb.now("T")))
        ev_sx = None
        for g, (j, bank, ev) in enumerate(sx):
            ev_sx = kb.bump("A")
            ACT(STG0[0:1, tg(g)], bank[0:1, :], AF.Copy, waits=[ev], inc=ev_sx)
            pool.done(j, ev_sx)
        sq_banks = [pool.get() for _ in range(TG)]
        prev_mm = [None, None]
        ev_sq_act = None
        for i, k in enumerate(korder):
            sqb = SQ if i % 2 == 0 else SQ2
            ev_sq_act = kb.bump("A")
            ACT(sqb[:, :], XRES[:, k, :], AF.Square,
                waits=[x_evs[k], prev_mm[i % 2], guard], inc=ev_sq_act)
            for g, (j, bank, free) in enumerate(sq_banks):
                w = [ev_sq_act] + ([free] if i == 0 else [])
                MM(bank[0:1, :], ONESC[:], sqb[:, tg(g)],
                   start=(i == 0), stop=(i == DC - 1), waits=w,
                   inc=(kb.bump("T") if g == TG - 1 else None))
            prev_mm[i % 2] = kb.now("T")
        ev_sq_mm = kb.now("T")
        ev_sqe = None
        for g, (j, bank, free) in enumerate(sq_banks):
            ev_sqe = kb.bump("A")
            ACT(STG1[0:1, tg(g)], bank[0:1, :], AF.Copy,
                waits=[ev_sq_mm], inc=ev_sqe)
            pool.done(j, ev_sqe)
        ev_o1 = kb.dma("sync", statd[0:1, :], STG0[0:1, :], waits=[ev_sx])
        ev_o2 = kb.dma("sync", statd[1:2, :], STG1[0:1, :], waits=[ev_sqe])
        ev_d1 = kb.dma("sync", STW0[:, :],
                       statd[0:1, :].rearrange("a (p j) -> (a p) j", p=128),
                       waits=[ev_o1])
        ev_d2 = kb.dma("sync", STW1[:, :],
                       statd[1:2, :].rearrange("a (p j) -> (a p) j", p=128),
                       waits=[ev_o2])
        e1 = kb.bump("V")
        V("tensor_scalar_mul", STW0[:, :], STW0[:, :], 1.0 / D,
          waits=[ev_d1], inc=e1)
        e2 = kb.bump("V")
        V("tensor_scalar_mul", STW1[:, :], STW1[:, :], 1.0 / D,
          waits=[ev_d2], inc=e2)
        e3 = kb.bump("V")
        V("tensor_tensor", out=LVT[:, :], in0=STW0[:, :],
          in1=STW0[:, :], op=ALU.mult, waits=[e1], inc=e3)
        e4 = kb.bump("V")
        V("scalar_tensor_tensor", out=STW1[:, :], in0=LVT[:, :],
          scalar=-1.0, in1=STW1[:, :], op0=ALU.mult, op1=ALU.add,
          waits=[e2, e3], inc=e4)
        e5 = kb.bump("A")
        ACT(STW1[:, :], STW1[:, :], AF.Sqrt, bias=EPSC[:], waits=[e4, ev_eps], inc=e5)
        e6 = kb.bump("V")
        V("reciprocal", out=STW1[:, :], in_=STW1[:, :], waits=[e5], inc=e6)
        e7 = kb.bump("V")
        V("tensor_scalar_mul", STW0[:, :], STW0[:, :], -1.0,
          waits=[e1], inc=e7)
        ev_c1 = kb.dma("sync", statd2[0:1, :].rearrange("a (p j) -> (a p) j", p=128),
                       STW0[:, :], waits=[e7])
        ev_c2 = kb.dma("sync", statd2[1:2, :].rearrange("a (p j) -> (a p) j", p=128),
                       STW1[:, :], waits=[e6])
        ev_b1 = kb.dma("sync", STG0[0:1, :], statd2[0:1, :], waits=[ev_c1])
        ev_b2 = kb.dma("sync", STG1[0:1, :], statd2[1:2, :], waits=[ev_c2])
        ev_mu = ev_iv = None
        for g in range(TG):
            j, bank, free = pool.get()
            MM(bank[:], ONES1[:], STG0[0:1, tg(g)], start=True, stop=True,
               waits=[free, ev_b1, ev_ones1], inc=kb.bump("T"))
            ev = kb.now("T")
            ev_mu = kb.bump("A")
            ACT(MUR[:, tg(g)], bank[:], AF.Copy, waits=[ev], inc=ev_mu)
            pool.done(j, ev_mu)
        for g in range(TG):
            j, bank, free = pool.get()
            MM(bank[:], ONES1[:], STG1[0:1, tg(g)], start=True, stop=True,
               waits=[free, ev_b2], inc=kb.bump("T"))
            ev = kb.now("T")
            ev_iv = kb.bump("A")
            ACT(IVR[:, tg(g)], bank[:], AF.Copy, waits=[ev], inc=ev_iv)
            pool.done(j, ev_iv)
        ev_ln = e4
        for k in range(DC):
            ea = kb.bump("V")
            V("tensor_tensor", out=LTMP[:, :], in0=XRES[:, k, :], in1=MUR[:, :],
              op=ALU.add, waits=[ev_mu, x_evs[k], ev_ln], inc=ea)
            ev_ln = kb.bump("V")
            V("tensor_tensor", out=LNX[:, k, :], in0=LTMP[:, :], in1=IVR[:, :],
              op=ALU.mult, waits=[ev_iv, ea], inc=ev_ln)
        return ev_ln, ev_sq_mm

    AAS = SMALL
    WBUF = alloc("WBUF", [128, 8, 8, 128], BF16, offset=AAS)
    SGA = alloc("SGA", [128, T], F32, offset=AAS + 16384)
    DECAY = alloc("DECAY", [128, T], F32, offset=AAS + 24576)
    SELA = alloc("SELA", [128, T], BF16, offset=AAS + 32768)
    TINA = alloc("TINA", [128, T], BF16, offset=AAS + 36864)
    GATEA = alloc("GATEA", [128, T], BF16, offset=AAS + 40960)
    UPDA = alloc("UPDA", [128, T], BF16, offset=AAS + 45056)
    HA = alloc("HA", [128, T], BF16, offset=AAS + 49152)
    GATED = alloc("GATED", [128, DC, T], BF16, offset=AAS + 53248)
    AA_END = AAS + 86016

    def attn_a(ev_ln, ev_scr_free):
        ev_w = {}
        ev_wfree = {}

        def load_w(p, e):
            b = p * 2 + (e % 2)
            w = [ev_wfree.get((p, e), ev_scr_free)]
            ev = kb.dma("gpsimd",
                        WBUF[:, b, :, :].rearrange("p k j -> p (k j)"),
                        wa[p][e], waits=w)
            ev_w[(p, e)] = (b, ev)

        for p in range(4):
            load_w(p, 0)
            load_w(p, 1)

        prev = {"sga": None, "sel": None, "tin": None, "gate": None,
                "dec": None, "upd": None, "h": None}
        ev_gated = []
        for e in range(DC):
            proj_ev = {}
            for p in range(4):
                b, wev = ev_w[(p, e)]
                guard = {0: prev["sga"], 1: prev["sel"], 2: prev["tin"],
                         3: prev["gate"]}[p]
                last = None
                for g in range(TG):
                    j, bank, free = pool.get()
                    for k in range(DC):
                        w = [wev, ev_ln] + ([free] if k == 0 else [])
                        MM(bank[:], WBUF[:, b, k, :], LNX[:, k, tg(g)],
                           start=(k == 0), stop=(k == DC - 1), waits=w,
                           inc=(kb.bump("T") if k == DC - 1 else None))
                    evt = kb.now("T")
                    ea = kb.bump("A")
                    sl = (slice(None), tg(g))
                    dst = {0: SGA, 1: SELA, 2: TINA, 3: GATEA}[p]
                    fn = {0: AF.Sigmoid, 1: AF.Sigmoid, 2: AF.Tanh,
                          3: AF.Sigmoid}[p]
                    sc = -1.0 if p == 0 else None
                    ACT(dst[sl], bank[:], fn, scale=sc,
                        waits=[evt, guard], inc=ea)
                    pool.done(j, ea)
                    last = ea
                proj_ev[p] = last
                evt2 = kb.now("T")
                ev_wfree[(p, e + 2)] = evt2
                if e + 2 < DC:
                    load_w(p, e + 2)
            eL = kb.bump("A")
            ACT(SGA[:, :], SGA[:, :], AF.Ln, waits=[proj_ev[0]], inc=eL)
            ed = kb.bump("V")
            V("tensor_scalar", out=DECAY[:, :], in0=SGA[:, :], scalar1=-1e-4,
              scalar2=ABASE[:, e:e + 1], op0=ALU.min, op1=ALU.mult,
              waits=[eL, ev_ab, prev["dec"]], inc=ed)
            prev["sga"] = ed
            ede = kb.bump("A")
            ACT(DECAY[:, :], DECAY[:, :], AF.Exp, waits=[ed], inc=ede)
            eu = kb.bump("V")
            V("tensor_tensor", out=UPDA[:, :], in0=SELA[:, :], in1=TINA[:, :],
              op=ALU.mult, waits=[proj_ev[1], proj_ev[2], prev["upd"]], inc=eu)
            prev["sel"] = eu
            prev["tin"] = eu
            es = kb.bump("V")
            V("tensor_tensor_scan", out=HA[:, :], data0=DECAY[:, :],
              data1=UPDA[:, :], initial=0.0, op0=ALU.mult, op1=ALU.add,
              waits=[ede, eu, prev["h"]], inc=es)
            prev["dec"] = es
            prev["upd"] = es
            eg = kb.bump("V")
            V("tensor_tensor", out=GATED[:, e, :], in0=GATEA[:, :], in1=HA[:, :],
              op=ALU.mult, waits=[proj_ev[3], es], inc=eg)
            prev["gate"] = eg
            prev["h"] = eg
            ev_gated.append(eg)

        ev_wo = {}
        ev_wofree = {0: kb.now("T"), 1: kb.now("T")}

        def load_wo(dp):
            b = dp % 2
            ev = kb.dma("gpsimd",
                        WBUF[:, b, :, :].rearrange("p k j -> p (k j)"),
                        wao[dp], waits=[ev_wofree[dp]])
            ev_wo[dp] = (b, ev)

        load_wo(0)
        load_wo(1)
        res_evs = [None] * DC
        for dp in range(DC):
            b, wev = ev_wo[dp]
            for g in range(TG):
                j, bank, free = pool.get()
                for k in range(DC):
                    w = [wev, ev_gated[k]] + ([free] if k == 0 else [])
                    MM(bank[:], WBUF[:, b, k, :], GATED[:, k, tg(g)],
                       start=(k == 0), stop=(k == DC - 1), waits=w,
                       inc=(kb.bump("T") if k == DC - 1 else None))
                evt = kb.now("T")
                ev_res = kb.bump("V")
                V("tensor_tensor", out=XRES[:, dp, tg(g)],
                  in0=XRES[:, dp, tg(g)], in1=bank[:], op=ALU.add,
                  waits=[evt], inc=ev_res)
                pool.done(j, ev_res)
            res_evs[dp] = ev_res
            ev_wofree[dp + 2] = kb.now("T")
            if dp + 2 < DC:
                load_wo(dp + 2)
        return res_evs

    FFS = SMALL
    PHW = alloc("PHW", [128, 8, T], BF16, offset=FFS)
    LNXTM = alloc("LNXTM", [128, 16, 1024], BF16, offset=FFS + 32768)
    SUMD = alloc("SUMD", [128, 16, 64], F32, offset=FFS + 65536)
    RCPD = alloc("RCPD", [128, 16, 64], F32, offset=FFS + 69632)
    RCPC = alloc("RCPC", [128, 16, 32], F32, offset=FFS + 73728)
    NRMX = alloc("NRMX", [128, 16], F32, offset=FFS + 75776)
    RMX4 = alloc("RMX4", [128, 4], F32, offset=FFS + 75840)
    XPS = [alloc("XPS0", [128, 1024], F32, offset=FFS + 75904),
           alloc("XPS1", [128, 1024], F32, offset=FFS + 80000)]
    assert FFS + 84096 <= CON
    EDC = alloc("EDC", [128, 16, T], BF16, offset=0)
    XT = alloc("XT", [128, 2, 1024], F32, offset=65536)
    XP = alloc("XP", [128, 2, 1024], BF16, offset=65536 + 8192)
    XST = alloc("XST", [128, 8], F32, offset=65536 + 12288)
    YTL = alloc("YTL", [32, 8, 1024], BF16, offset=65536 + 12800)
    YTL2 = alloc("YTL2", [64, 4, 1024], BF16, offset=65536 + 12800)
    EXPWB = [alloc("EXPWB0", [128, 8, 1024], BF16, offset=0),
             alloc("EXPWB1", [128, 8, 1024], BF16, offset=16384)]
    XSCR = alloc("XSCR", [128, 1024], F32, offset=FFS + 75904)
    XPT = alloc("XPT", [128, 8, 256], BF16, offset=FFS + 80000)
    CSM = alloc("CSM", [128, 16, T], BF16, offset=FFS)
    YTS = alloc("YTS", [128, 16, 1024], BF16, offset=65536)

    ONES1B = alloc("ONES1B", [1, 128], BF16, offset=co[0])
    co[0] += 256
    ev_ones1b = kb.dma("gpsimd", ONES1B[:], ones1)

    def ffn(call, ev_ln, ev_scr, x_ev):
        x_last = x_ev[-1] if isinstance(x_ev, list) else x_ev
        ev_spill = kb.dma("sync", xspill, XRES[:].rearrange("p k t -> p (k t)"),
                          waits=[x_last])
        ev_phw_k = []
        for k in range(DC):
            ev_phw_k.append(kb.dma("sync" if k % 2 == 0 else "scalar",
                                   PHW[:, k, :], phi_bf[k],
                                   waits=[x_last, ev_scr, ev_ln, ev_phibf]))
        ev_edc = []
        ev_tmc = []
        ev_ptr_free = None
        ev_sumd = [None] * 16
        ev_rcpd = [None] * 16
        ev_rcpc = [None] * 16
        ev_dnorm = [None] * 16
        for c in range(16):
            csl = slice(c * 128, (c + 1) * 128)
            banks = []
            for g in range(TG):
                j, bank, free = pool.get()
                for k in range(DC):
                    w = [ev_phw_k[k], ev_ln] + ([free] if k == 0 else [])
                    MM(bank[:], LNX[:, k, csl], PHW[:, k, tg(g)],
                       start=(k == 0), stop=False, waits=w)
                MM(bank[:], ONES1B[:], PHIB[0:1, tg(g)], start=False, stop=True,
                   waits=[ev_ones1b], inc=kb.bump("T"))
                banks.append((j, bank, kb.now("T")))
            ev_e = None
            for g, (j, bank, evt) in enumerate(banks):
                ev_e = kb.bump("A")
                ACT(EDC[:, c, tg(g)], bank[:], AF.Exp,
                    waits=[evt, ev_spill if c == 0 else None,
                           ev_ln if c == 0 else None,
                           ev_edc[-1] if c > 0 else None], inc=ev_e)
                pool.done(j, ev_e)
            ev_edc.append(ev_e)
            es1 = kb.bump("V")
            V("reduce_sum", out=SUMD[:, c, :],
              in_=EDC[:, c, :].rearrange("p (n q) -> p n q", n=64),
              axis=AX.X, waits=[ev_e], inc=es1)
            ev_sumd[c] = es1
            es2 = kb.bump("V")
            V("reduce_sum", out=RCPC[:, c, :],
              in_=EDC[:, c, :].rearrange("p (n q) -> p q n", n=64),
              axis=AX.X, waits=[ev_e], inc=es2)
            er1 = kb.bump("V")
            V("reciprocal", out=RCPD[:, c, :], in_=SUMD[:, c, :],
              waits=[es1], inc=er1)
            ev_rcpd[c] = er1
            er2 = kb.bump("V")
            V("reciprocal", out=RCPC[:, c, :], in_=RCPC[:, c, :],
              waits=[es2], inc=er2)
            ev_rcpc[c] = er2
            ed = kb.bump("V")
            V("tensor_tensor",
              out=EDC[:, c, :].rearrange("p (n q) -> p n q", n=64),
              in0=EDC[:, c, :].rearrange("p (n q) -> p n q", n=64),
              in1=RCPD[:, c, :].rearrange("p n -> p n ()").broadcast_to([128, 64, 32]),
              op=ALU.mult, waits=[er1], inc=ed)
            ev_dnorm[c] = ed
            for k in range(DC):
                def tb(e, o=ptr[:, k * 128:(k + 1) * 128], i=LNX[:, k, csl]):
                    return e.transpose(o, i, IDENT[:])
                kb.emit("tensor", tb, waits=[ev_ln, ev_ptr_free],
                        inc=(kb.bump("T") if k == DC - 1 else None))
            evt = kb.now("T")
            etm = kb.bump("A")
            ACT(LNXTM[:, c, :], ptr[:], AF.Copy, waits=[evt], inc=etm)
            ev_ptr_free = etm
            ev_tmc.append(etm)
        ev_xps = [None, None]
        ev_rs_dma = None
        for sc in range(16):
            for dg in range(2):
                j, bank, free = pool.get()
                for c in range(16):
                    w = [ev_dnorm[c], ev_tmc[c]] + ([free] if c == 0 else [])
                    MM(bank[:], EDC[:, c, sc * 128:(sc + 1) * 128],
                       LNXTM[:, c, dg * 512:(dg + 1) * 512],
                       start=(c == 0), stop=(c == 15), waits=w,
                       inc=(kb.bump("T") if c == 15 else None))
                evt = kb.now("T")
                eb = kb.bump("A")
                ACT(XPS[sc % 2][:, dg * 512:(dg + 1) * 512], bank[:], AF.Copy,
                    waits=[evt, ev_xps[sc % 2]], inc=eb)
                pool.done(j, eb)
            ev_rs_dma = kb.dma("sync", rs_in[sc * 128:(sc + 1) * 128, :],
                               XPS[sc % 2][:, :], waits=[eb])
            ev_xps[sc % 2] = ev_rs_dma
        ev_cc1 = kb.bump("CC")

        def cc1(e):
            return e.collective_compute(
                "ReduceScatter", ALU.add, replica_groups=[list(range(NCORES))],
                ins=[rs_in[:]], outs=[rs_out[:]])
        kb.emit("gpsimd", cc1, waits=kb.qdone("sync"), inc=ev_cc1)

        ev_cn = []
        for c in range(16):
            e1 = kb.bump("V")
            V("tensor_tensor",
              out=EDC[:, c, :].rearrange("p (n q) -> p n q", n=64),
              in0=EDC[:, c, :].rearrange("p (n q) -> p n q", n=64),
              in1=SUMD[:, c, :].rearrange("p n -> p n ()").broadcast_to([128, 64, 32]),
              op=ALU.mult, waits=[ev_sumd[c], kb.now("T")], inc=e1)
            e2 = kb.bump("V")
            V("tensor_tensor",
              out=EDC[:, c, :].rearrange("p (n q) -> p n q", n=64),
              in0=EDC[:, c, :].rearrange("p (n q) -> p n q", n=64),
              in1=RCPC[:, c, :].rearrange("p q -> p () q").broadcast_to([128, 64, 32]),
              op=ALU.mult, waits=[ev_rcpc[c], e1], inc=e2)
            ev_cn.append(e2)

        ev_csm_c = [None] * 16
        ev_rel = [None] * DC
        for c in range(16):
            ecs = None
            for half in range(2):
                for i in range(8):
                    sc = half * 8 + i
                    def tb(e, o=ptr[:, i * 128:(i + 1) * 128],
                           i_=EDC[:, c, sc * 128:(sc + 1) * 128]):
                        return e.transpose(o, i_, IDENT[:])
                    kb.emit("tensor", tb,
                            waits=[ev_cn[c], ev_ptr_free],
                            inc=(kb.bump("T") if i == 7 else None))
                evt = kb.now("T")
                ecs = kb.bump("A")
                ACT(CSM[:, half * 8:(half + 1) * 8, c * 128:(c + 1) * 128],
                    ptr[:].rearrange("p (s j) -> p s j", s=8), AF.Copy,
                    waits=[evt], inc=ecs)
                ev_ptr_free = ecs
            ev_csm_c[c] = ecs

        ev_xt = kb.dma("sync", XT[:], rs_out.rearrange("(q p) h -> p q h", p=128),
                       waits=[ev_cc1])
        ec = kb.bump("V")
        V("tensor_scalar", out=XT[:].rearrange("p q h -> p (q h)"),
          in0=XT[:].rearrange("p q h -> p (q h)"), scalar1=-33000.0,
          scalar2=65000.0, op0=ALU.max, op1=ALU.min, waits=[ev_xt], inc=ec)
        ev_xp = []
        for q in range(2):
            es = kb.bump("V")
            V("reduce_sum", out=XST[:, q:q + 1], in_=XT[:, q, :], axis=AX.X,
              waits=[ec], inc=es)
            ea = kb.bump("A")
            ACT(XSCR[:, :], XT[:, q, :], AF.Square, accum=XST[:, q + 2:q + 3],
                waits=[ec, ev_xp[-1] if q else None], inc=ea)
            e1 = kb.bump("V")
            V("tensor_scalar_mul", XST[:, q:q + 1], XST[:, q:q + 1], 1.0 / D,
              waits=[es], inc=e1)
            e2 = kb.bump("V")
            V("tensor_scalar_mul", XST[:, q + 2:q + 3], XST[:, q + 2:q + 3],
              1.0 / D, waits=[ea], inc=e2)
            e3 = kb.bump("V")
            V("scalar_tensor_tensor", out=XST[:, q + 4:q + 5],
              in0=XST[:, q:q + 1], scalar=XST[:, q:q + 1], in1=XST[:, q + 2:q + 3],
              op0=ALU.mult, op1=ALU.subtract, waits=[e1, e2], inc=e3)
            e3b = kb.bump("V")
            V("tensor_scalar_mul", XST[:, q + 4:q + 5], XST[:, q + 4:q + 5], -1.0,
              waits=[e3], inc=e3b)
            e4 = kb.bump("A")
            ACT(XST[:, q + 4:q + 5], XST[:, q + 4:q + 5], AF.Sqrt, bias=EPSC[:],
                waits=[e3b], inc=e4)
            e5 = kb.bump("V")
            V("reciprocal", out=XST[:, q + 4:q + 5], in_=XST[:, q + 4:q + 5],
              waits=[e4], inc=e5)
            e6 = kb.bump("V")
            V("tensor_scalar", out=XSCR[:, :], in0=XT[:, q, :],
              scalar1=XST[:, q:q + 1], scalar2=XST[:, q + 4:q + 5],
              op0=ALU.subtract, op1=ALU.mult, waits=[e5, ea], inc=e6)
            e7 = kb.bump("A")
            ACT(XP[:, q, :], XSCR[:, :], AF.Tanh, waits=[e6], inc=e7)
            ev_xp.append(e7)
        ev_xpt = None
        for q in range(2):
            for k in range(DC):
                def tb(e, o=ptr[:, k * 128:(k + 1) * 128],
                       i=XP[:, q, k * 128:(k + 1) * 128]):
                    return e.transpose(o, i, IDENT[:])
                kb.emit("tensor", tb, waits=[ev_xp[q], ev_ptr_free],
                        inc=(kb.bump("T") if k == DC - 1 else None))
            evt = kb.now("T")
            ev_xpt = kb.bump("A")
            ACT(XPT[:, :, q * 128:(q + 1) * 128],
                ptr[:].rearrange("p (k s) -> p k s", k=8), AF.Copy,
                waits=[evt], inc=ev_xpt)
            ev_ptr_free = ev_xpt
        ev_ew = {}
        ev_ewfree = {0: None, 1: None}

        def load_ew(n):
            b = n % 2
            w = [ev_ewfree[b], ev_ewbf[n], ev_csm_c[4 * b + 3]]
            e1 = kb.dma("sync", EXPWB[b][:, 0:4, :].rearrange("p k h -> p (k h)"),
                        wexp_bf[n][:, 0:4096], waits=w)
            e2 = kb.dma("scalar", EXPWB[b][:, 4:8, :].rearrange("p k h -> p (k h)"),
                        wexp_bf[n][:, 4096:8192], waits=w)
            ev_ew[n] = (b, [e1, e2])

        load_ew(0)
        load_ew(1)
        ev_ytl = None
        for n in range(NEXP_LOC):
            b, wev = ev_ew[n]
            for dg in range(2):
                j, bank, free = pool.get()
                for k in range(DC):
                    w = wev + [ev_xpt] + ([free] if k == 0 else [])
                    MM(bank[0:32, :], XPT[:, k, n * 32:(n + 1) * 32],
                       EXPWB[b][:, k, dg * 512:(dg + 1) * 512],
                       start=(k == 0), stop=False, waits=w)
                MM(bank[0:32, :], SEL8[0:8, n * 32:(n + 1) * 32],
                   EXPB[0:8, dg * 512:(dg + 1) * 512], start=False,
                   stop=True, waits=[ev_sel8], inc=kb.bump("T"))
                evt = kb.now("T")
                ev_ytl = kb.bump("A")
                ACT(YTL[0:32, n, dg * 512:(dg + 1) * 512], bank[0:32, :],
                    AF.Copy, waits=[evt], inc=ev_ytl)
                pool.done(j, ev_ytl)
            ev_ewfree[b] = kb.now("T")
            if n + 2 < NEXP_LOC:
                load_ew(n + 2)
        ev_agin = kb.dma("sync", ag_in.rearrange("(n p) h -> p n h", p=32),
                         YTL[:], waits=[ev_ytl])
        ev_cc2 = kb.bump("CC")

        def cc2(e):
            return e.collective_compute(
                "AllGather", ALU.bypass, replica_groups=[list(range(NCORES))],
                ins=[ag_in[:]], outs=[ag_out[:]])
        kb.emit("gpsimd", cc2, waits=[ev_agin], inc=ev_cc2)
        ev_yts = kb.dma("sync", YTS[:],
                        ag_out.rearrange("(s p) h -> p s h", p=128),
                        waits=[ev_cc2])
        ev_ew_done = kb.now("T")
        ev_rel = [None] * DC
        for dp in range(DC):
            w = [ev_ew_done] if dp < 4 else [ev_csm_c[2 * dp + 1]]
            ev_rel[dp] = kb.dma("scalar", XRES[:, dp, :],
                                xspill[:, dp * T:(dp + 1) * T], waits=w)
        res_evs = [None] * DC
        for g in range(TG):
            need = ev_csm_c[4 * g + 3]
            for dp in range(DC):
                j, bank, free = pool.get()
                for sc in range(16):
                    w = [ev_yts, need] + ([free] if sc == 0 else [])
                    MM(bank[:], YTS[:, sc, dp * 128:(dp + 1) * 128],
                       CSM[:, sc, tg(g)], start=(sc == 0), stop=(sc == 15),
                       waits=w, inc=(kb.bump("T") if sc == 15 else None))
                evt = kb.now("T")
                ev_res = kb.bump("V")
                V("tensor_tensor", out=XRES[:, dp, tg(g)],
                  in0=XRES[:, dp, tg(g)], in1=bank[:], op=ALU.add,
                  waits=[evt, ev_rel[dp], res_evs[dp]], inc=ev_res)
                pool.done(j, ev_res)
                res_evs[dp] = ev_res
        return res_evs

    ABS = SMALL
    XSYO = alloc("XSYO", [128, 8, T], BF16, offset=ABS)
    ZB = alloc("ZB", [128, 8, T], BF16, offset=ABS + 32768)
    XZ = alloc("XZ", [128, T], F32, offset=ABS + 65536)
    CV = alloc("CV", [128, T], F32, offset=ABS + 73728)
    WBB = alloc("WBB", [128, 16, 128], BF16, offset=ABS + 81920)
    BRP = alloc("BRP", [128, 4, T], BF16, offset=ABS + 32768)
    CRP = alloc("CRP", [128, 4, T], BF16, offset=ABS + 49152)
    MDB = [alloc("MDB0", [128, T], F32, offset=ABS + 65536),
           alloc("MDB1", [128, T], F32, offset=ABS + 73728)]
    XDBW = alloc("XDBW", [96, T], BF16, offset=65536)
    ABF = [alloc("AB0", [128, T], F32, offset=65536 + 4096),
           alloc("AB1", [128, T], F32, offset=65536 + 12288)]
    BB2 = [alloc("BB0", [128, T], BF16, offset=65536 + 20480),
           alloc("BB1", [128, T], BF16, offset=65536 + 24576)]
    DU = alloc("DU", [128, 8, T], BF16, offset=0)
    HBB = alloc("HBB2", [128, T], BF16, offset=32768)
    PB2 = [alloc("PB0", [128, T], BF16, offset=36864),
           alloc("PB1", [128, T], BF16, offset=40960)]
    SGB = XZ
    MDP = CV

    def attn_b(ev_ln, ev_scr, x_ev):
        x_evs = x_ev if isinstance(x_ev, list) else [x_ev] * DC
        ev_wb = {}
        ev_wbfree = {}

        def load_wbin(i):
            b = i % 2
            ev = kb.dma("gpsimd",
                        WBB[:, 0:8, :].rearrange("p k j -> p (k j)")
                        if b == 0 else WBB[:, 8:16, :].rearrange("p k j -> p (k j)"),
                        wbin[i], waits=[ev_wbfree.get(i)])
            ev_wb[i] = (b, ev)

        ev_wbfree[0] = ev_scr
        ev_wbfree[1] = ev_scr
        load_wbin(0)
        load_wbin(1)
        ev_xs = [None] * 8
        ev_zb = [None] * 8
        ev_conv_prev = None
        ev_xz_evac_prev = None
        for i in range(16):
            b, wev = ev_wb[i]
            wslab = WBB[:, 0:8, :] if b == 0 else WBB[:, 8:16, :]
            ev_ev = None
            for g in range(TG):
                j, bank, free = pool.get()
                for k in range(DC):
                    w = [wev, ev_ln] + ([free] if k == 0 else [])
                    MM(bank[:], wslab[:, k, :], LNX[:, k, tg(g)],
                       start=(k == 0), stop=(k == DC - 1), waits=w,
                       inc=(kb.bump("T") if k == DC - 1 else None))
                evt = kb.now("T")
                ev_ev = kb.bump("A")
                ACT(XZ[:, tg(g)], bank[:], AF.Copy,
                    waits=[evt, ev_conv_prev], inc=ev_ev)
                pool.done(j, ev_ev)
            ev_wbfree[i + 2] = kb.now("T")
            if i + 2 < 16:
                load_wbin(i + 2)
            kt = KXT if i < 8 else KZT
            kc = i if i < 8 else i - 8
            e1 = kb.bump("V")
            V("tensor_scalar_mul", CV[:, :], XZ[:, :], kt[:, kc * 4 + 1:kc * 4 + 2],
              waits=[ev_ev, ev_conv_prev], inc=e1)
            e2 = kb.bump("V")
            V("scalar_tensor_tensor", out=CV[:, 1:T], in0=XZ[:, 0:T - 1],
              scalar=kt[:, kc * 4 + 0:kc * 4 + 1], in1=CV[:, 1:T],
              op0=ALU.mult, op1=ALU.add, waits=[e1], inc=e2)
            e3 = kb.bump("V")
            V("scalar_tensor_tensor", out=CV[:, 0:T - 1], in0=XZ[:, 1:T],
              scalar=kt[:, kc * 4 + 2:kc * 4 + 3], in1=CV[:, 0:T - 1],
              op0=ALU.mult, op1=ALU.add, waits=[e2], inc=e3)
            e4 = kb.bump("V")
            V("scalar_tensor_tensor", out=CV[:, 0:T - 2], in0=XZ[:, 2:T],
              scalar=kt[:, kc * 4 + 3:kc * 4 + 4], in1=CV[:, 0:T - 2],
              op0=ALU.mult, op1=ALU.add, waits=[e3], inc=e4)
            es = kb.bump("A")
            dst = XSYO[:, i, :] if i < 8 else ZB[:, i - 8, :]
            ACT(dst, CV[:, :], AF.Silu, waits=[e4], inc=es)
            ev_conv_prev = es
            if i < 8:
                ev_xs[i] = es
            else:
                ev_zb[i - 8] = es

        ev_wo = {}
        ev_wofree = {0: kb.now("T"), 1: kb.now("T")}

        def load_wbo(dp, half_only=None):
            b = dp % 2
            sl = WBB[:, 0:8, :] if b == 0 else WBB[:, 8:16, :]
            src = wbo[dp].rearrange("p (i j) -> p i j", i=16)
            src = src[:, 8:16, :] if half_only == "z" else src[:, 0:8, :]
            ev = kb.dma("gpsimd", sl.rearrange("p k j -> p (k j)"),
                        src.rearrange("p i j -> p (i j)"),
                        waits=[ev_wofree.get(dp)])
            ev_wo[dp] = (b, ev)

        ev_wofree[0] = kb.now("T")
        ev_wofree[1] = kb.now("T")
        load_wbo(0, "z")
        load_wbo(1, "z")
        ev_zres = None
        for dp in range(DC):
            b, wev = ev_wo[dp]
            wslab = WBB[:, 0:8, :] if b == 0 else WBB[:, 8:16, :]
            for g in range(TG):
                j, bank, free = pool.get()
                for ii in range(8):
                    w = [wev, ev_zb[ii]] + ([free] if ii == 0 else [])
                    MM(bank[:], wslab[:, ii, :], ZB[:, ii, tg(g)],
                       start=(ii == 0), stop=(ii == 7), waits=w,
                       inc=(kb.bump("T") if ii == 7 else None))
                evt = kb.now("T")
                ev_zres = kb.bump("V")
                V("scalar_tensor_tensor", out=XRES[:, dp, tg(g)], in0=bank[:],
                  scalar=BOUTB[:, dp:dp + 1], in1=XRES[:, dp, tg(g)],
                  op0=ALU.add, op1=ALU.add, waits=[evt, x_evs[dp]], inc=ev_zres)
                pool.done(j, ev_zres)
            ev_wofree[dp + 2] = kb.now("T")
            if dp + 2 < DC:
                load_wbo(dp + 2, "z")

        ev_spill = kb.dma("sync", xspill, XRES[:].rearrange("p k t -> p (k t)"),
                          waits=[ev_zres])

        ev_xdb = None
        for g in range(TG):
            j, bank, free = pool.get()
            for k in range(DC):
                w = [ev_xs[k]] + ([free] if k == 0 else [])
                MM(bank[0:96, :], WXPB[:, k * 96:(k + 1) * 96],
                   XSYO[:, k, tg(g)], start=(k == 0), stop=(k == DC - 1),
                   waits=w, inc=(kb.bump("T") if k == DC - 1 else None))
            evt = kb.now("T")
            ev_xdb = kb.bump("A")
            ACT(XDBW[0:96, tg(g)], bank[0:96, :], AF.Copy,
                waits=[evt], inc=ev_xdb)
            pool.done(j, ev_xdb)

        ev_md_sp = [None] * 8
        ev_du = [None] * 8
        ev_yo = [None] * 8
        ev_sg_prev = ev_conv_prev
        ev_mdp_prev = ev_conv_prev
        ev_mdsp_prev = None
        for k in range(DC):
            banks = []
            for g in range(TG):
                j, bank, free = pool.get()
                MM(bank[:], WDT[0:64, k * 128:(k + 1) * 128],
                   XDBW[0:64, tg(g)], start=True, stop=True,
                   waits=[ev_xdb] + ([free] if True else []),
                   inc=kb.bump("T"))
                banks.append((j, bank, kb.now("T")))
            ev_sg = None
            for g, (j, bank, evt) in enumerate(banks):
                ev_sg = kb.bump("A")
                ACT(SGB[:, tg(g)], bank[:], AF.Sigmoid, scale=-1.0,
                    bias=NBDT[:, k:k + 1],
                    waits=[evt, ev_nbdt, ev_sg_prev], inc=ev_sg)
                pool.done(j, ev_sg)
            ev_mdp = kb.bump("A")
            ACT(MDP[:, :], SGB[:, :], AF.Ln,
                waits=[ev_sg, ev_mdp_prev, ev_mdsp_prev], inc=ev_mdp)
            ev_sg_prev = ev_mdp
            ev_md_sp[k] = kb.dma("sync", mdspill[:, k * T:(k + 1) * T],
                                 MDP[:, :], waits=[ev_mdp])
            ev_mdsp_prev = ev_md_sp[k]
            ev_du[k] = kb.bump("V")
            V("tensor_tensor", out=DU[:, k, :], in0=MDP[:, :],
              in1=XSYO[:, k, :], op=ALU.mult,
              waits=[ev_mdp, ev_xs[k], ev_spill], inc=ev_du[k])
            ev_mdp_prev = ev_du[k]
            ev_yo[k] = kb.bump("V")
            V("tensor_scalar_mul", XSYO[:, k, :], XSYO[:, k, :],
              DPAR[:, k:k + 1], waits=[ev_du[k]], inc=ev_yo[k])

        ev_mdb_init = [kb.now("A"), kb.now("V")]
        ev_mdb_q = kb.qdone("sync")
        ev_brp = [None] * 4
        ev_crp = [None] * 4
        ev_brp_cons = [None] * 4
        ev_crp_cons = [None] * 4
        ev_mdb_cons = ev_mdb_init
        ev_ab_cons = [None, None]
        ev_bb_cons = [None, None]
        ev_pb_cons = [None, None]
        ev_hb_cons = None
        ev_yo_prev = list(ev_yo)
        for grp in range(4):
            for m in range(4):
                n = grp * 4 + m
                for rep, (slab, dstb, evl) in enumerate(
                        ((SELBB, BRP, ev_brp), (SELCB, CRP, ev_crp))):
                    evv = None
                    for g in range(TG):
                        j, bank, free = pool.get()
                        MM(bank[:], slab[64:96, n * 128:(n + 1) * 128],
                           XDBW[64:96, tg(g)], start=True, stop=True,
                           waits=[ev_xdb, free,
                                  (ev_brp_cons if rep == 0 else ev_crp_cons)[m]],
                           inc=kb.bump("T"))
                        evt = kb.now("T")
                        evv = kb.bump("A")
                        ACT(dstb[:, m, tg(g)], bank[:], AF.Copy,
                            waits=[evt], inc=evv)
                        pool.done(j, evv)
                    evl[m] = evv
            for k in range(DC):
                mb = k % 2
                ev_mdl = kb.dma("sync", MDB[mb][:, :],
                                mdspill[:, k * T:(k + 1) * T],
                                waits=[ev_md_sp[k], ev_mdb_cons[mb]] + ev_mdb_q)
                ybanks = []
                for m in range(4):
                    n = grp * 4 + m
                    ab = m % 2
                    ea = kb.bump("A")
                    ACT(ABF[ab][:, :], MDB[mb][:, :], AF.Exp,
                        scale=APOS[:, k * 16 + n:k * 16 + n + 1],
                        waits=[ev_mdl, ev_aposx, ev_ab_cons[ab]], inc=ea)
                    eb = kb.bump("V")
                    V("tensor_tensor", out=BB2[ab][:, :], in0=DU[:, k, :],
                      in1=BRP[:, m, :], op=ALU.mult,
                      waits=[ev_du[k], ev_brp[m], ev_bb_cons[ab]], inc=eb)
                    if k == DC - 1:
                        ev_brp_cons[m] = eb
                    eh = kb.bump("V")
                    V("tensor_tensor_scan", out=HBB[:, :], data0=ABF[ab][:, :],
                      data1=BB2[ab][:, :], initial=0.0, op0=ALU.mult,
                      op1=ALU.add, waits=[ea, eb, ev_hb_cons], inc=eh)
                    ev_ab_cons[ab] = eh
                    ev_bb_cons[ab] = eh
                    ep = kb.bump("V")
                    V("tensor_tensor", out=PB2[ab][:, :], in0=HBB[:, :],
                      in1=CRP[:, m, :], op=ALU.mult,
                      waits=[eh, ev_crp[m], ev_pb_cons[ab]], inc=ep)
                    if k == DC - 1:
                        ev_crp_cons[m] = ep
                    ev_hb_cons = ep
                    if m == 0:
                        for g in range(TG):
                            j, bank, free = pool.get()
                            ybanks.append((j, bank))
                            MM(bank[:], NIDENT[:], PB2[ab][:, tg(g)],
                               start=True, stop=False, waits=[ep, free],
                               inc=(kb.bump("T") if g == TG - 1 else None))
                    else:
                        for g in range(TG):
                            j, bank = ybanks[g]
                            MM(bank[:], NIDENT[:], PB2[ab][:, tg(g)],
                               start=False, stop=(m == 3), waits=[ep],
                               inc=(kb.bump("T") if g == TG - 1 else None))
                    ev_pb_cons[ab] = kb.now("T")
                ev_mdb_cons[mb] = kb.now("A")
                evt = kb.now("T")
                for g in range(TG):
                    j, bank = ybanks[g]
                    ey = kb.bump("V")
                    V("tensor_tensor", out=XSYO[:, k, tg(g)],
                      in0=XSYO[:, k, tg(g)], in1=bank[:], op=ALU.add,
                      waits=[evt, ev_yo_prev[k]], inc=ey)
                    pool.done(j, ey)
                    ev_yo_prev[k] = ey


        ev_xrel = kb.dma("sync", XRES[:].rearrange("p k t -> p (k t)"), xspill,
                         waits=[kb.now("V"), kb.now("T")])
        ev_wo2 = {}
        ev_wo2free = {0: kb.now("T"), 1: kb.now("T")}

        def load_wbo2(dp):
            b = dp % 2
            sl = WBB[:, 0:8, :] if b == 0 else WBB[:, 8:16, :]
            src = wbo[dp].rearrange("p (i j) -> p i j", i=16)[:, 0:8, :]
            ev = kb.dma("gpsimd", sl.rearrange("p k j -> p (k j)"),
                        src.rearrange("p i j -> p (i j)"),
                        waits=[ev_wo2free.get(dp)])
            ev_wo2[dp] = (b, ev)

        load_wbo2(0)
        load_wbo2(1)
        res_evs = [None] * DC
        for dp in range(DC):
            b, wev = ev_wo2[dp]
            wslab = WBB[:, 0:8, :] if b == 0 else WBB[:, 8:16, :]
            for g in range(TG):
                j, bank, free = pool.get()
                for ii in range(8):
                    w = [wev, ev_yo_prev[ii]] + ([free] if ii == 0 else [])
                    MM(bank[:], wslab[:, ii, :], XSYO[:, ii, tg(g)],
                       start=(ii == 0), stop=(ii == 7), waits=w,
                       inc=(kb.bump("T") if ii == 7 else None))
                evt = kb.now("T")
                ev_res = kb.bump("V")
                V("tensor_tensor", out=XRES[:, dp, tg(g)],
                  in0=XRES[:, dp, tg(g)], in1=bank[:], op=ALU.add,
                  waits=[evt, ev_xrel], inc=ev_res)
                pool.done(j, ev_res)
            res_evs[dp] = ev_res
            ev_wo2free[dp + 2] = kb.now("T")
            if dp + 2 < DC:
                load_wbo2(dp + 2)
        return res_evs

    x_ev = ev_x
    if PHASES >= 1:
        ev_ln1, ev_scr = layernorm(x_ev)
        if PHASES >= 2:
            x_ev = attn_a(ev_ln1, ev_scr)
    ev_phibf = kb.dma("gpsimd", phi_bf[:], phi[:])
    ev_ewbf = [kb.dma("gpsimd", wexp_bf[n], expw[n]) for n in range(NEXP_LOC)]
    if PHASES >= 3:
        ev_ln2, ev_scr2 = layernorm(x_ev)
        if PHASES >= 4:
            x_ev = ffn(0, ev_ln2, ev_scr2, x_ev)
    if PHASES >= 5:
        ev_ln3, ev_scr3 = layernorm(x_ev, guard=kb.now("T"))
        if PHASES >= 6:
            x_ev = attn_b(ev_ln3, ev_scr3, x_ev)
    if PHASES >= 7:
        ev_ln4, ev_scr4 = layernorm(x_ev, guard=kb.now("T"))
        if PHASES >= 8:
            x_ev = ffn(1, ev_ln4, ev_scr4, x_ev)

    if DEBUG_OUT == "lnx":
        ev_store = kb.dma("gpsimd", y_out.rearrange("(k p) t -> p k t", p=128),
                          LNX[:], waits=[kb.now("V"), kb.now("A"), kb.now("T")])
    else:
        xw = x_ev if isinstance(x_ev, list) else [x_ev]
        ev_store = kb.dma("sync", y_out.rearrange("(k p) t -> p k t", p=128),
                          XRES[:], waits=list(xw))
    kb.emit("sync", lambda e: e.nop(), waits=[ev_store])

    with nc.Block() as block:
        for eng in ENGS:
            fns = kb.q[eng]
            getattr(block, eng)(lambda e, fns=fns: [f(e) for f in fns])
    return kb



def _prep_core(c, I):
    pk = np.ascontiguousarray
    d = {}
    d["x"] = pk(I["L0_attn_a_x"][c].T)

    def packW(Wt, ec, kc):
        out = np.empty((ec, 128, kc * 128), np.float32)
        for e in range(ec):
            sl = Wt[:, e * 128:(e + 1) * 128]
            out[e] = sl.reshape(kc, 128, 128).transpose(1, 0, 2).reshape(128, kc * 128)
        return pk(out)

    d["wa0"] = packW(I["L0_attn_a_W_delta"].T, 8, 8)
    d["wa1"] = packW(I["L0_attn_a_W_select"].T, 8, 8)
    d["wa2"] = packW(I["L0_attn_a_W_in"].T, 8, 8)
    d["wa3"] = packW(I["L0_attn_a_W_gate"].T, 8, 8)
    d["wao"] = packW(I["L0_attn_a_W_out"].T, 8, 8)
    d["a_log"] = pk(I["L0_attn_a_log_a"].reshape(8, 128).T)

    d["phi"] = pk(I["L0_ffn_phi_w"].T.reshape(8, 128, 2048))
    d["phib"] = pk(I["L0_ffn_phi_b"].reshape(1, 2048))
    ew = I["L0_ffn_expert_w"][c * 8:(c + 1) * 8]
    d["expw"] = pk(ew.reshape(8, 8, 128, 1024).transpose(0, 2, 1, 3).reshape(8, 128, 8192))
    d["expb"] = pk(I["L0_ffn_expert_b"][c * 8:(c + 1) * 8])

    d["wbin"] = packW(I["L0_attn_b_W_in"], 16, 8)
    d["kx"] = pk(I["L0_attn_b_K_x"][:, 0, :].reshape(4, 8, 128).transpose(2, 1, 0).reshape(128, 32))
    d["kz"] = pk(I["L0_attn_b_K_z"][:, 0, :].reshape(4, 8, 128).transpose(2, 1, 0).reshape(128, 32))
    d["wxp"] = pk(I["L0_attn_b_W_xp"].reshape(8, 128, 96).transpose(1, 0, 2).reshape(128, 768))
    d["wdt"] = pk(I["L0_attn_b_W_dt"])
    d["b_dt"] = pk(I["L0_attn_b_b_dt"].reshape(8, 128).T)
    d["a_logb"] = pk(I["L0_attn_b_A_log"].reshape(8, 128, 16).transpose(1, 0, 2).reshape(128, 128))
    d["d_par"] = pk(I["L0_attn_b_D_param"].reshape(8, 128).T)
    d["b_outb"] = pk(I["L0_attn_b_b_out"].reshape(8, 128).T)
    d["wbo"] = packW(I["L0_attn_b_W_out"], 8, 16)

    d["identm"] = np.eye(128, dtype=np.float32)
    d["nidentm"] = -np.eye(128, dtype=np.float32)
    d["ones1"] = np.ones((1, 128), np.float32)
    d["onescol"] = np.ones((128, 1), np.float32)
    selb = np.zeros((128, 2048), np.float32)
    selc = np.zeros((128, 2048), np.float32)
    for n in range(16):
        selb[64 + n, n * 128:(n + 1) * 128] = 1.0
        selc[80 + n, n * 128:(n + 1) * 128] = 1.0
    d["selb"] = selb
    d["selc"] = selc
    s8 = np.zeros((8, 256), np.float32)
    for n in range(8):
        s8[n, n * 32:(n + 1) * 32] = 1.0
    d["sel8"] = s8
    return d


def kernel(**inputs):
    kb = build_program()
    in_maps = [_prep_core(c, inputs) for c in range(NCORES)]
    res = run_bass_kernel_spmd(kb.nc, in_maps, core_ids=list(range(NCORES)))
    kb.stack.close()
    return np.stack([np.ascontiguousarray(res.results[c]["y"].T)
                     for c in range(NCORES)])


# version 13
# speedup vs baseline: 1.0150x; 1.0150x over previous
import contextlib
import numpy as np
import concourse.bass as bass
import concourse.mybir as mybir
from concourse.bass_utils import run_bass_kernel_spmd

F32 = mybir.dt.float32
BF16 = mybir.dt.bfloat16
AF = mybir.ActivationFunctionType
ALU = mybir.AluOpType
AX = mybir.AxisListType

NCORES = 8
T = 2048
D = 1024
DC = 8
TG = 4
NTG = 512
NEXP_LOC = 8
EPS = 1e-5

PHASES = 99
DEBUG_OUT = None

ENGS = ("sync", "scalar", "vector", "tensor", "gpsimd")


class KB:
    def __init__(self):
        self.nc = bass.Bass("TRN2", target_bir_lowering=False)
        self.stack = contextlib.ExitStack()
        self.q = {e: [] for e in ENGS}
        self.sems = {}
        self.ct = {}
        self.deltas = {}
        self.waited = {e: {} for e in ENGS}

    def sem(self, name):
        if name not in self.sems:
            self.sems[name] = self.stack.enter_context(self.nc.semaphore(name))
            self.ct[name] = 0
        return self.sems[name]

    def bump(self, name, k=1):
        self.sem(name)
        self.ct[name] += k
        ev = (name, self.ct[name])
        self.deltas[ev] = k
        return ev

    def now(self, name):
        self.sem(name)
        return (name, self.ct[name])

    def emit(self, eng, build, waits=(), inc=None):
        w = []
        for ev in waits:
            if ev is None:
                continue
            s, v = ev
            if v <= 0 or self.waited[eng].get(s, -1) >= v:
                continue
            self.waited[eng][s] = v
            w.append((self.sems[s], v))
        if inc is not None:
            isem = self.sems[inc[0]]
            idelta = self.deltas.pop(inc)
        else:
            isem = idelta = None

        def fn(e, build=build, w=tuple(w), isem=isem, idelta=idelta):
            for s, v in w:
                e.wait_ge(s, v)
            ins = build(e)
            if isem is not None:
                ins.then_inc(isem, idelta)
        self.q[eng].append(fn)

    _dma_rot = {}

    def dma(self, eng, out, in_, waits=()):
        base = {"sync": "dS", "gpsimd": "dG", "scalar": "dC"}[eng]
        r = self._dma_rot.get(eng, 0)
        self._dma_rot[eng] = r + 1
        qsem = f"{base}{r % 3}"
        prev = self.now(qsem)
        ev = self.bump(qsem, 16)

        def build(e, out=out, in_=in_):
            return e.dma_start(out=out, in_=in_)
        self.emit(eng, build, waits=list(waits) + [prev], inc=ev)
        return ev

    def qdone(self, eng):
        base = {"sync": "dS", "gpsimd": "dG", "scalar": "dC"}[eng]
        return [self.now(f"{base}{i}") for i in range(3)]


def build_program():
    kb = KB()
    nc = kb.nc
    st = kb.stack

    def din(name, shape, dtype=F32):
        return nc.dram_tensor(name, list(shape), dtype, kind="ExternalInput").ap()

    x_in = din("x", (D, T))
    y_out = nc.dram_tensor("y", [D, T], F32, kind="ExternalOutput").ap()

    wa = [din(f"wa{p}", (8, 128, 1024)) for p in range(4)]
    wao = din("wao", (8, 128, 1024))
    a_log = din("a_log", (128, 8))

    phi = din("phi", (8, 128, 2048))
    phib = din("phib", (1, 2048))
    expw = din("expw", (8, 128, 8192))
    expb = din("expb", (8, 1024))

    wbin = din("wbin", (16, 128, 1024))
    kx = din("kx", (128, 32))
    kz = din("kz", (128, 32))
    wxp = din("wxp", (128, 768))
    wdt = din("wdt", (64, 1024))
    b_dt = din("b_dt", (128, 8))
    a_logb = din("a_logb", (128, 128))
    d_par = din("d_par", (128, 8))
    b_outb = din("b_outb", (128, 8))
    wbo = din("wbo", (8, 128, 2048))

    identm = din("identm", (128, 128))
    nidentm = din("nidentm", (128, 128))
    ones1 = din("ones1", (1, 128))
    ones1n = din("ones1n", (1, 128))
    onescol = din("onescol", (128, 1))
    selb = din("selb", (128, 2048))
    selc = din("selc", (128, 2048))
    sel8 = din("sel8", (8, 256))

    rs_in = nc.dram_tensor("rs_in", [2048, 1024], F32).ap()
    rs_out = nc.dram_tensor("rs_out", [256, 1024], F32).ap()
    ag_in = nc.dram_tensor("ag_in", [256, 1024], BF16).ap()
    ag_out = nc.dram_tensor("ag_out", [2048, 1024], BF16, addr_space="Shared").ap()
    xspill = nc.dram_tensor("xspill", [128, DC * T], F32).ap()
    mdspill = nc.dram_tensor("mdspill", [128, DC * T], F32).ap()
    statd = nc.dram_tensor("statd", [2, 2048], F32).ap()
    statd2 = nc.dram_tensor("statd2", [2, 2048], F32).ap()
    wexp_bf = nc.dram_tensor("wexp_bf", [8, 128, 8192], BF16).ap()
    phi_bf = nc.dram_tensor("phi_bf", [8, 128, 2048], BF16).ap()

    probe = st.enter_context(nc.sbuf_tensor("probe", [128, 32], mybir.dt.uint8))
    BASE = (nc.lookup_mloc(probe).addr + 32 + 2048 + 31) // 32 * 32

    def alloc(name, shape, dtype, offset):
        return nc.alloc_sbuf_tensor_at(name, shape, dtype, offset=BASE + offset)

    XRES = alloc("XRES", [128, DC, T], F32, offset=0)
    LNX = alloc("LNX", [128, DC, T], BF16, offset=65536)
    CON = 195072

    co = [CON]

    def cbuf(name, free_shape, dtype):
        h = alloc(name, [128] + list(free_shape), dtype, offset=co[0])
        per = int(np.prod(free_shape)) * mybir.dt.size(dtype)
        co[0] += (per + 31) // 32 * 32
        return h

    IDENT = cbuf("IDENT", [128], BF16)
    NIDENT = cbuf("NIDENT", [128], BF16)
    ONESC = cbuf("ONESC", [1], BF16)
    ABASE = cbuf("ABASE", [8], F32)
    ALOGT = cbuf("ALOGT", [8], F32)
    APOS = cbuf("APOS", [128], F32)
    NBDT = cbuf("NBDT", [8], F32)
    BDTT = cbuf("BDTT", [8], F32)
    DPAR = cbuf("DPAR", [8], F32)
    BOUTB = cbuf("BOUTB", [8], F32)
    KXT = cbuf("KXT", [32], F32)
    KZT = cbuf("KZT", [32], F32)
    EPSC = cbuf("EPSC", [1], F32)
    SELBB = cbuf("SELBB", [2048], BF16)
    SELCB = cbuf("SELCB", [2048], BF16)
    assert BASE + co[0] <= 229376, (BASE, co[0])

    SCR = 98304
    ONES1 = alloc("ONES1", [1, 128], F32, offset=SCR)
    ONES1N = alloc("ONES1N", [1, 128], F32, offset=co[0])
    co[0] += 512
    assert BASE + co[0] <= 229376
    PHIB = alloc("PHIB", [1, 2048], BF16, offset=SCR + 512)
    WDT = alloc("WDT", [64, 1024], BF16, offset=SCR + 4608)
    EXPB = alloc("EXPB", [8, 1024], BF16, offset=SCR + 6656)
    WXPB = alloc("WXPB", [128, 768], BF16, offset=SCR + 8704)
    SEL8 = alloc("SEL8", [8, 256], BF16, offset=SCR + 10240)
    SMALL = SCR + 10752

    pbanks = [st.enter_context(nc.psum_tensor(f"pb{i}", [128, 512], F32))
              for i in range(7)]
    ptr = st.enter_context(nc.psum_tensor("ptr", [128, 1024], BF16))

    class Pool:
        def __init__(self, banks):
            self.b = banks
            self.ev = [None] * len(banks)
            self.i = 0

        def get(self):
            j = self.i % len(self.b)
            self.i += 1
            return j, self.b[j], self.ev[j]

        def done(self, j, ev):
            self.ev[j] = ev

    pool = Pool(pbanks)

    def MM(bank_ap, lhsT, rhs, start, stop, waits=(), inc=None):
        def b(e, a=bank_ap, l=lhsT, r=rhs, s1=start, s2=stop):
            return e.matmul(a, l, r, start=s1, stop=s2)
        kb.emit("tensor", b, waits=waits, inc=inc)

    def ACT(out, in_, func, scale=None, bias=None, accum=None, waits=(), inc=None):
        def b(e, out=out, in_=in_, func=func, scale=scale, bias=bias, accum=accum):
            kw = {}
            if scale is not None:
                kw["scale"] = scale
            if bias is not None:
                kw["bias"] = bias
            if accum is not None:
                kw["accum_out"] = accum
            return e.activation(out=out, in_=in_, func=func, **kw)
        kb.emit("scalar", b, waits=waits, inc=inc)

    def V(meth, *args, waits=(), inc=None, **kwargs):
        def b(e, meth=meth, args=args, kwargs=kwargs):
            return getattr(e, meth)(*args, **kwargs)
        kb.emit("vector", b, waits=waits, inc=inc)

    def G(meth, *args, waits=(), inc=None, **kwargs):
        def b(e, meth=meth, args=args, kwargs=kwargs):
            return getattr(e, meth)(*args, **kwargs)
        kb.emit("gpsimd", b, waits=waits, inc=inc)

    def tg(g):
        return slice(g * NTG, (g + 1) * NTG)

    ev_x = kb.dma("sync", XRES[:],
                  x_in.rearrange("(k p) t -> p k t", p=128))
    e_alog = kb.dma("sync", ALOGT[:], a_log)
    e_bdt = kb.dma("sync", BDTT[:], b_dt)
    e_apos = kb.dma("sync", APOS[:], a_logb)
    kb.dma("sync", DPAR[:], d_par)
    kb.dma("sync", BOUTB[:], b_outb)
    kb.dma("sync", KXT[:], kx)
    kb.dma("sync", KZT[:], kz)
    ev_ones1 = kb.dma("sync", ONES1[:], ones1)
    ev_ones1n = kb.dma("sync", ONES1N[:], ones1n)

    kb.dma("gpsimd", IDENT[:], identm)
    kb.dma("gpsimd", NIDENT[:], nidentm)
    ev_onesc = kb.dma("gpsimd", ONESC[:], onescol)
    kb.dma("gpsimd", SELBB[:], selb)
    kb.dma("gpsimd", SELCB[:], selc)
    ev_sel8 = kb.dma("gpsimd", SEL8[:], sel8)
    kb.dma("gpsimd", PHIB[:], phib)
    kb.dma("gpsimd", WDT[:], wdt)
    kb.dma("gpsimd", EXPB[:], expb)
    kb.dma("gpsimd", WXPB[:], wxp)

    ev_ab = kb.bump("A")
    ACT(ABASE[:], ALOGT[:], AF.Sigmoid, waits=[e_alog], inc=ev_ab)
    ev_aposx = kb.bump("A")
    ACT(APOS[:], APOS[:], AF.Exp, waits=[e_apos], inc=ev_aposx)
    ev_nbdt = kb.bump("V")
    V("tensor_scalar_mul", NBDT[:], BDTT[:], -1.0, waits=[e_bdt], inc=ev_nbdt)
    ev_eps = kb.bump("V")
    V("memset", EPSC[:], EPS, inc=ev_eps)

    LNS = SMALL
    XB = alloc("XB", [128, DC, T], BF16, offset=LNS + 4096)
    STG0 = alloc("STG0", [1, T], F32, offset=LNS + 36864)
    STG1 = alloc("STG1", [1, T], F32, offset=LNS + 45056)
    STW0 = alloc("STW0", [128, 16], F32, offset=LNS + 53248)
    STW1 = alloc("STW1", [128, 16], F32, offset=LNS + 53312)
    MUR = alloc("MUR", [128, T], F32, offset=LNS + 53376)
    IVR = alloc("IVR", [128, T], BF16, offset=LNS + 61568)
    LTMP = alloc("LTMP", [128, T], BF16, offset=LNS + 69760)
    LVT = alloc("LVT", [128, 16], F32, offset=LNS + 77952)
    SQ = alloc("SQ", [128, T], BF16, offset=65536)
    SQ2 = alloc("SQ2", [128, T], BF16, offset=69632)
    LN_END = LNS + 78016

    def layernorm(x_ev, guard=None):
        x_evs = x_ev if isinstance(x_ev, list) else [x_ev] * DC
        korder = list(range(DC))
        ev_xb = [None] * DC
        for k in korder:
            ev_xb[k] = kb.bump("A")
            ACT(XB[:, k, :], XRES[:, k, :], AF.Copy,
                waits=[x_evs[k], guard], inc=ev_xb[k])
        sx = []
        for g in range(TG):
            j, bank, free = pool.get()
            for i, k in enumerate(korder):
                w = [ev_xb[k], ev_onesc] + ([free] if i == 0 else [])
                MM(bank[0:1, :], ONESC[:], XB[:, k, tg(g)],
                   start=(i == 0), stop=(i == DC - 1),
                   waits=w, inc=(kb.bump("T") if i == DC - 1 else None))
            sx.append((j, bank, kb.now("T")))
        ev_sx = None
        for g, (j, bank, ev) in enumerate(sx):
            ev_sx = kb.bump("A")
            ACT(STG0[0:1, tg(g)], bank[0:1, :], AF.Copy, waits=[ev], inc=ev_sx)
            pool.done(j, ev_sx)
        sq_banks = [pool.get() for _ in range(TG)]
        prev_mm = [None, None]
        ev_sq_act = None
        for i, k in enumerate(korder):
            sqb = SQ if i % 2 == 0 else SQ2
            ev_sq_act = kb.bump("A")
            ACT(sqb[:, :], XRES[:, k, :], AF.Square,
                waits=[x_evs[k], prev_mm[i % 2], guard], inc=ev_sq_act)
            for g, (j, bank, free) in enumerate(sq_banks):
                w = [ev_sq_act] + ([free] if i == 0 else [])
                MM(bank[0:1, :], ONESC[:], sqb[:, tg(g)],
                   start=(i == 0), stop=(i == DC - 1), waits=w,
                   inc=(kb.bump("T") if g == TG - 1 else None))
            prev_mm[i % 2] = kb.now("T")
        ev_sq_mm = kb.now("T")
        ev_sqe = None
        for g, (j, bank, free) in enumerate(sq_banks):
            ev_sqe = kb.bump("A")
            ACT(STG1[0:1, tg(g)], bank[0:1, :], AF.Copy,
                waits=[ev_sq_mm], inc=ev_sqe)
            pool.done(j, ev_sqe)
        ev_o1 = kb.dma("sync", statd[0:1, :], STG0[0:1, :], waits=[ev_sx])
        ev_o2 = kb.dma("sync", statd[1:2, :], STG1[0:1, :], waits=[ev_sqe])
        ev_d1 = kb.dma("sync", STW0[:, :],
                       statd[0:1, :].rearrange("a (p j) -> (a p) j", p=128),
                       waits=[ev_o1])
        ev_d2 = kb.dma("sync", STW1[:, :],
                       statd[1:2, :].rearrange("a (p j) -> (a p) j", p=128),
                       waits=[ev_o2])
        e3 = kb.bump("V")
        V("tensor_tensor", out=LVT[:, :], in0=STW0[:, :],
          in1=STW0[:, :], op=ALU.mult, waits=[ev_d1], inc=e3)
        e2 = ev_d2
        e4 = kb.bump("V")
        V("scalar_tensor_tensor", out=STW1[:, :], in0=LVT[:, :],
          scalar=-1.0, in1=STW1[:, :], op0=ALU.mult, op1=ALU.add,
          waits=[e2, e3], inc=e4)
        e5 = kb.bump("A")
        ACT(STW1[:, :], STW1[:, :], AF.Sqrt, bias=EPSC[:], waits=[e4, ev_eps], inc=e5)
        e6 = kb.bump("V")
        V("reciprocal", out=STW1[:, :], in_=STW1[:, :], waits=[e5], inc=e6)
        ev_c1 = kb.dma("sync", statd2[0:1, :].rearrange("a (p j) -> (a p) j", p=128),
                       STW0[:, :], waits=[e3])
        ev_c2 = kb.dma("sync", statd2[1:2, :].rearrange("a (p j) -> (a p) j", p=128),
                       STW1[:, :], waits=[e6])
        ev_b1 = kb.dma("sync", STG0[0:1, :], statd2[0:1, :], waits=[ev_c1])
        ev_b2 = kb.dma("sync", STG1[0:1, :], statd2[1:2, :], waits=[ev_c2])
        ev_mu = ev_iv = None
        for g in range(TG):
            j, bank, free = pool.get()
            MM(bank[:], ONES1N[:], STG0[0:1, tg(g)], start=True, stop=True,
               waits=[free, ev_b1, ev_ones1n], inc=kb.bump("T"))
            ev = kb.now("T")
            ev_mu = kb.bump("A")
            ACT(MUR[:, tg(g)], bank[:], AF.Copy, waits=[ev], inc=ev_mu)
            pool.done(j, ev_mu)
        for g in range(TG):
            j, bank, free = pool.get()
            MM(bank[:], ONES1[:], STG1[0:1, tg(g)], start=True, stop=True,
               waits=[free, ev_b2, ev_ones1], inc=kb.bump("T"))
            ev = kb.now("T")
            ev_iv = kb.bump("A")
            ACT(IVR[:, tg(g)], bank[:], AF.Copy, waits=[ev], inc=ev_iv)
            pool.done(j, ev_iv)
        ev_ln = e4
        for k in range(DC):
            ea = kb.bump("V")
            V("tensor_tensor", out=LTMP[:, :], in0=XRES[:, k, :], in1=MUR[:, :],
              op=ALU.add, waits=[ev_mu, x_evs[k], ev_ln], inc=ea)
            ev_ln = kb.bump("V")
            V("tensor_tensor", out=LNX[:, k, :], in0=LTMP[:, :], in1=IVR[:, :],
              op=ALU.mult, waits=[ev_iv, ea], inc=ev_ln)
        return ev_ln, ev_sq_mm

    AAS = SMALL
    WBUF = alloc("WBUF", [128, 8, 8, 128], BF16, offset=AAS)
    SGA = alloc("SGA", [128, T], F32, offset=AAS + 16384)
    DECAY = alloc("DECAY", [128, T], F32, offset=AAS + 24576)
    SELA = alloc("SELA", [128, T], BF16, offset=AAS + 32768)
    TINA = alloc("TINA", [128, T], BF16, offset=AAS + 36864)
    GATEA = alloc("GATEA", [128, T], BF16, offset=AAS + 40960)
    UPDA = alloc("UPDA", [128, T], BF16, offset=AAS + 45056)
    HA = alloc("HA", [128, T], BF16, offset=AAS + 49152)
    GATED = alloc("GATED", [128, DC, T], BF16, offset=AAS + 53248)
    AA_END = AAS + 86016

    def attn_a(ev_ln, ev_scr_free):
        ev_w = {}
        ev_wfree = {}

        def load_w(p, e):
            b = p * 2 + (e % 2)
            w = [ev_wfree.get((p, e), ev_scr_free)]
            ev = kb.dma("gpsimd",
                        WBUF[:, b, :, :].rearrange("p k j -> p (k j)"),
                        wa[p][e], waits=w)
            ev_w[(p, e)] = (b, ev)

        for p in range(4):
            load_w(p, 0)
            load_w(p, 1)

        prev = {"sga": None, "sel": None, "tin": None, "gate": None,
                "dec": None, "upd": None, "h": None}
        ev_gated = []
        for e in range(DC):
            proj_ev = {}
            for p in range(4):
                b, wev = ev_w[(p, e)]
                guard = {0: prev["sga"], 1: prev["sel"], 2: prev["tin"],
                         3: prev["gate"]}[p]
                last = None
                for g in range(TG):
                    j, bank, free = pool.get()
                    for k in range(DC):
                        w = [wev, ev_ln] + ([free] if k == 0 else [])
                        MM(bank[:], WBUF[:, b, k, :], LNX[:, k, tg(g)],
                           start=(k == 0), stop=(k == DC - 1), waits=w,
                           inc=(kb.bump("T") if k == DC - 1 else None))
                    evt = kb.now("T")
                    ea = kb.bump("A")
                    sl = (slice(None), tg(g))
                    dst = {0: SGA, 1: SELA, 2: TINA, 3: GATEA}[p]
                    fn = {0: AF.Sigmoid, 1: AF.Sigmoid, 2: AF.Tanh,
                          3: AF.Sigmoid}[p]
                    sc = -1.0 if p == 0 else None
                    ACT(dst[sl], bank[:], fn, scale=sc,
                        waits=[evt, guard], inc=ea)
                    pool.done(j, ea)
                    last = ea
                proj_ev[p] = last
                evt2 = kb.now("T")
                ev_wfree[(p, e + 2)] = evt2
                if e + 2 < DC:
                    load_w(p, e + 2)
            eL = kb.bump("A")
            ACT(SGA[:, :], SGA[:, :], AF.Ln, waits=[proj_ev[0]], inc=eL)
            ed = kb.bump("V")
            V("tensor_scalar", out=DECAY[:, :], in0=SGA[:, :], scalar1=-1e-4,
              scalar2=ABASE[:, e:e + 1], op0=ALU.min, op1=ALU.mult,
              waits=[eL, ev_ab, prev["dec"]], inc=ed)
            prev["sga"] = ed
            ede = kb.bump("A")
            ACT(DECAY[:, :], DECAY[:, :], AF.Exp, waits=[ed], inc=ede)
            eu = kb.bump("V")
            V("tensor_tensor", out=UPDA[:, :], in0=SELA[:, :], in1=TINA[:, :],
              op=ALU.mult, waits=[proj_ev[1], proj_ev[2], prev["upd"]], inc=eu)
            prev["sel"] = eu
            prev["tin"] = eu
            es = kb.bump("V")
            V("tensor_tensor_scan", out=HA[:, :], data0=DECAY[:, :],
              data1=UPDA[:, :], initial=0.0, op0=ALU.mult, op1=ALU.add,
              waits=[ede, eu, prev["h"]], inc=es)
            prev["dec"] = es
            prev["upd"] = es
            eg = kb.bump("V")
            V("tensor_tensor", out=GATED[:, e, :], in0=GATEA[:, :], in1=HA[:, :],
              op=ALU.mult, waits=[proj_ev[3], es], inc=eg)
            prev["gate"] = eg
            prev["h"] = eg
            ev_gated.append(eg)

        ev_wo = {}
        ev_wofree = {0: kb.now("T"), 1: kb.now("T")}

        def load_wo(dp):
            b = dp % 2
            ev = kb.dma("gpsimd",
                        WBUF[:, b, :, :].rearrange("p k j -> p (k j)"),
                        wao[dp], waits=[ev_wofree[dp]])
            ev_wo[dp] = (b, ev)

        load_wo(0)
        load_wo(1)
        res_evs = [None] * DC
        for dp in range(DC):
            b, wev = ev_wo[dp]
            for g in range(TG):
                j, bank, free = pool.get()
                for k in range(DC):
                    w = [wev, ev_gated[k]] + ([free] if k == 0 else [])
                    MM(bank[:], WBUF[:, b, k, :], GATED[:, k, tg(g)],
                       start=(k == 0), stop=(k == DC - 1), waits=w,
                       inc=(kb.bump("T") if k == DC - 1 else None))
                evt = kb.now("T")
                ev_res = kb.bump("V")
                V("tensor_tensor", out=XRES[:, dp, tg(g)],
                  in0=XRES[:, dp, tg(g)], in1=bank[:], op=ALU.add,
                  waits=[evt], inc=ev_res)
                pool.done(j, ev_res)
            res_evs[dp] = ev_res
            ev_wofree[dp + 2] = kb.now("T")
            if dp + 2 < DC:
                load_wo(dp + 2)
        return res_evs

    FFS = SMALL
    PHW = alloc("PHW", [128, 8, T], BF16, offset=FFS)
    LNXTM = alloc("LNXTM", [128, 16, 1024], BF16, offset=FFS + 32768)
    SUMD = alloc("SUMD", [128, 16, 64], F32, offset=FFS + 65536)
    RCPD = alloc("RCPD", [128, 16, 64], F32, offset=FFS + 69632)
    RCPC = alloc("RCPC", [128, 16, 32], F32, offset=FFS + 73728)
    NRMX = alloc("NRMX", [128, 16], F32, offset=FFS + 75776)
    RMX4 = alloc("RMX4", [128, 4], F32, offset=FFS + 75840)
    XPS = [alloc("XPS0", [128, 1024], F32, offset=FFS + 75904),
           alloc("XPS1", [128, 1024], F32, offset=FFS + 80000)]
    assert FFS + 84096 <= CON
    EDC = alloc("EDC", [128, 16, T], BF16, offset=0)
    XT = alloc("XT", [128, 2, 1024], F32, offset=65536)
    XP = alloc("XP", [128, 2, 1024], BF16, offset=65536 + 8192)
    XST = alloc("XST", [128, 8], F32, offset=65536 + 12288)
    YTL = alloc("YTL", [32, 8, 1024], BF16, offset=65536 + 12800)
    YTL2 = alloc("YTL2", [64, 4, 1024], BF16, offset=65536 + 12800)
    EXPWB = [alloc("EXPWB0", [128, 8, 1024], BF16, offset=0),
             alloc("EXPWB1", [128, 8, 1024], BF16, offset=16384)]
    XSCR = alloc("XSCR", [128, 1024], F32, offset=FFS + 75904)
    XPT = alloc("XPT", [128, 8, 256], BF16, offset=FFS + 80000)
    CSM = alloc("CSM", [128, 16, T], BF16, offset=FFS)
    YTS = alloc("YTS", [128, 16, 1024], BF16, offset=65536)

    ONES1B = alloc("ONES1B", [1, 128], BF16, offset=co[0])
    co[0] += 256
    ev_ones1b = kb.dma("gpsimd", ONES1B[:], ones1)

    def ffn(call, ev_ln, ev_scr, x_ev):
        x_last = x_ev[-1] if isinstance(x_ev, list) else x_ev
        ev_spill = kb.dma("sync", xspill, XRES[:].rearrange("p k t -> p (k t)"),
                          waits=[x_last])
        ev_phw_k = []
        for k in range(DC):
            ev_phw_k.append(kb.dma("sync" if k % 2 == 0 else "scalar",
                                   PHW[:, k, :], phi_bf[k],
                                   waits=[x_last, ev_scr, ev_ln, ev_phibf]))
        ev_edc = []
        ev_tmc = []
        ev_ptr_free = None
        ev_sumd = [None] * 16
        ev_rcpd = [None] * 16
        ev_rcpc = [None] * 16
        ev_dnorm = [None] * 16
        for c in range(16):
            csl = slice(c * 128, (c + 1) * 128)
            banks = []
            for g in range(TG):
                j, bank, free = pool.get()
                for k in range(DC):
                    w = [ev_phw_k[k], ev_ln] + ([free] if k == 0 else [])
                    MM(bank[:], LNX[:, k, csl], PHW[:, k, tg(g)],
                       start=(k == 0), stop=False, waits=w)
                MM(bank[:], ONES1B[:], PHIB[0:1, tg(g)], start=False, stop=True,
                   waits=[ev_ones1b], inc=kb.bump("T"))
                banks.append((j, bank, kb.now("T")))
            ev_e = None
            for g, (j, bank, evt) in enumerate(banks):
                ev_e = kb.bump("A")
                ACT(EDC[:, c, tg(g)], bank[:], AF.Exp,
                    waits=[evt, ev_spill if c == 0 else None,
                           ev_ln if c == 0 else None,
                           ev_edc[-1] if c > 0 else None], inc=ev_e)
                pool.done(j, ev_e)
            ev_edc.append(ev_e)
            es1 = kb.bump("V")
            V("reduce_sum", out=SUMD[:, c, :],
              in_=EDC[:, c, :].rearrange("p (n q) -> p n q", n=64),
              axis=AX.X, waits=[ev_e], inc=es1)
            ev_sumd[c] = es1
            es2 = kb.bump("V")
            V("reduce_sum", out=RCPC[:, c, :],
              in_=EDC[:, c, :].rearrange("p (n q) -> p q n", n=64),
              axis=AX.X, waits=[ev_e], inc=es2)
            er1 = kb.bump("V")
            V("reciprocal", out=RCPD[:, c, :], in_=SUMD[:, c, :],
              waits=[es1], inc=er1)
            ev_rcpd[c] = er1
            er2 = kb.bump("V")
            V("reciprocal", out=RCPC[:, c, :], in_=RCPC[:, c, :],
              waits=[es2], inc=er2)
            ev_rcpc[c] = er2
            ed = kb.bump("V")
            V("tensor_tensor",
              out=EDC[:, c, :].rearrange("p (n q) -> p n q", n=64),
              in0=EDC[:, c, :].rearrange("p (n q) -> p n q", n=64),
              in1=RCPD[:, c, :].rearrange("p n -> p n ()").broadcast_to([128, 64, 32]),
              op=ALU.mult, waits=[er1], inc=ed)
            ev_dnorm[c] = ed
            for k in range(DC):
                def tb(e, o=ptr[:, k * 128:(k + 1) * 128], i=LNX[:, k, csl]):
                    return e.transpose(o, i, IDENT[:])
                kb.emit("tensor", tb, waits=[ev_ln, ev_ptr_free],
                        inc=(kb.bump("T") if k == DC - 1 else None))
            evt = kb.now("T")
            etm = kb.bump("A")
            ACT(LNXTM[:, c, :], ptr[:], AF.Copy, waits=[evt], inc=etm)
            ev_ptr_free = etm
            ev_tmc.append(etm)
        ev_xps = [None, None]
        ev_rs_dma = None
        for sc in range(16):
            for dg in range(2):
                j, bank, free = pool.get()
                for c in range(16):
                    w = [ev_dnorm[c], ev_tmc[c]] + ([free] if c == 0 else [])
                    MM(bank[:], EDC[:, c, sc * 128:(sc + 1) * 128],
                       LNXTM[:, c, dg * 512:(dg + 1) * 512],
                       start=(c == 0), stop=(c == 15), waits=w,
                       inc=(kb.bump("T") if c == 15 else None))
                evt = kb.now("T")
                eb = kb.bump("A")
                ACT(XPS[sc % 2][:, dg * 512:(dg + 1) * 512], bank[:], AF.Copy,
                    waits=[evt, ev_xps[sc % 2]], inc=eb)
                pool.done(j, eb)
            ev_rs_dma = kb.dma("sync", rs_in[sc * 128:(sc + 1) * 128, :],
                               XPS[sc % 2][:, :], waits=[eb])
            ev_xps[sc % 2] = ev_rs_dma
        ev_cc1 = kb.bump("CC")

        def cc1(e):
            return e.collective_compute(
                "ReduceScatter", ALU.add, replica_groups=[list(range(NCORES))],
                ins=[rs_in[:]], outs=[rs_out[:]])
        kb.emit("gpsimd", cc1, waits=kb.qdone("sync"), inc=ev_cc1)

        ev_cn = []
        for c in range(16):
            e1 = kb.bump("V")
            V("tensor_tensor",
              out=EDC[:, c, :].rearrange("p (n q) -> p n q", n=64),
              in0=EDC[:, c, :].rearrange("p (n q) -> p n q", n=64),
              in1=SUMD[:, c, :].rearrange("p n -> p n ()").broadcast_to([128, 64, 32]),
              op=ALU.mult, waits=[ev_sumd[c], kb.now("T")], inc=e1)
            e2 = kb.bump("V")
            V("tensor_tensor",
              out=EDC[:, c, :].rearrange("p (n q) -> p n q", n=64),
              in0=EDC[:, c, :].rearrange("p (n q) -> p n q", n=64),
              in1=RCPC[:, c, :].rearrange("p q -> p () q").broadcast_to([128, 64, 32]),
              op=ALU.mult, waits=[ev_rcpc[c], e1], inc=e2)
            ev_cn.append(e2)

        ev_csm_c = [None] * 16
        ev_rel = [None] * DC
        for c in range(16):
            ecs = None
            for half in range(2):
                for i in range(8):
                    sc = half * 8 + i
                    def tb(e, o=ptr[:, i * 128:(i + 1) * 128],
                           i_=EDC[:, c, sc * 128:(sc + 1) * 128]):
                        return e.transpose(o, i_, IDENT[:])
                    kb.emit("tensor", tb,
                            waits=[ev_cn[c], ev_ptr_free],
                            inc=(kb.bump("T") if i == 7 else None))
                evt = kb.now("T")
                ecs = kb.bump("A")
                ACT(CSM[:, half * 8:(half + 1) * 8, c * 128:(c + 1) * 128],
                    ptr[:].rearrange("p (s j) -> p s j", s=8), AF.Copy,
                    waits=[evt], inc=ecs)
                ev_ptr_free = ecs
            ev_csm_c[c] = ecs

        ev_xt = kb.dma("sync", XT[:], rs_out.rearrange("(q p) h -> p q h", p=128),
                       waits=[ev_cc1])
        ec = kb.bump("V")
        V("tensor_scalar", out=XT[:].rearrange("p q h -> p (q h)"),
          in0=XT[:].rearrange("p q h -> p (q h)"), scalar1=-33000.0,
          scalar2=65000.0, op0=ALU.max, op1=ALU.min, waits=[ev_xt], inc=ec)
        ev_xp = []
        for q in range(2):
            es = kb.bump("V")
            V("reduce_sum", out=XST[:, q:q + 1], in_=XT[:, q, :], axis=AX.X,
              waits=[ec], inc=es)
            ea = kb.bump("A")
            ACT(XSCR[:, :], XT[:, q, :], AF.Square, accum=XST[:, q + 2:q + 3],
                waits=[ec, ev_xp[-1] if q else None], inc=ea)
            e1 = kb.bump("V")
            V("tensor_scalar_mul", XST[:, q:q + 1], XST[:, q:q + 1], 1.0 / D,
              waits=[es], inc=e1)
            e2 = kb.bump("V")
            V("tensor_scalar_mul", XST[:, q + 2:q + 3], XST[:, q + 2:q + 3],
              1.0 / D, waits=[ea], inc=e2)
            e3 = kb.bump("V")
            V("scalar_tensor_tensor", out=XST[:, q + 4:q + 5],
              in0=XST[:, q:q + 1], scalar=XST[:, q:q + 1], in1=XST[:, q + 2:q + 3],
              op0=ALU.mult, op1=ALU.subtract, waits=[e1, e2], inc=e3)
            e3b = kb.bump("V")
            V("tensor_scalar_mul", XST[:, q + 4:q + 5], XST[:, q + 4:q + 5], -1.0,
              waits=[e3], inc=e3b)
            e4 = kb.bump("A")
            ACT(XST[:, q + 4:q + 5], XST[:, q + 4:q + 5], AF.Sqrt, bias=EPSC[:],
                waits=[e3b], inc=e4)
            e5 = kb.bump("V")
            V("reciprocal", out=XST[:, q + 4:q + 5], in_=XST[:, q + 4:q + 5],
              waits=[e4], inc=e5)
            e6 = kb.bump("V")
            V("tensor_scalar", out=XSCR[:, :], in0=XT[:, q, :],
              scalar1=XST[:, q:q + 1], scalar2=XST[:, q + 4:q + 5],
              op0=ALU.subtract, op1=ALU.mult, waits=[e5, ea], inc=e6)
            e7 = kb.bump("A")
            ACT(XP[:, q, :], XSCR[:, :], AF.Tanh, waits=[e6], inc=e7)
            ev_xp.append(e7)
        ev_xpt = None
        for q in range(2):
            for k in range(DC):
                def tb(e, o=ptr[:, k * 128:(k + 1) * 128],
                       i=XP[:, q, k * 128:(k + 1) * 128]):
                    return e.transpose(o, i, IDENT[:])
                kb.emit("tensor", tb, waits=[ev_xp[q], ev_ptr_free],
                        inc=(kb.bump("T") if k == DC - 1 else None))
            evt = kb.now("T")
            ev_xpt = kb.bump("A")
            ACT(XPT[:, :, q * 128:(q + 1) * 128],
                ptr[:].rearrange("p (k s) -> p k s", k=8), AF.Copy,
                waits=[evt], inc=ev_xpt)
            ev_ptr_free = ev_xpt
        ev_ew = {}
        ev_ewfree = {0: None, 1: None}

        def load_ew(n):
            b = n % 2
            w = [ev_ewfree[b], ev_ewbf[n], ev_csm_c[4 * b + 3]]
            e1 = kb.dma("sync", EXPWB[b][:, 0:4, :].rearrange("p k h -> p (k h)"),
                        wexp_bf[n][:, 0:4096], waits=w)
            e2 = kb.dma("scalar", EXPWB[b][:, 4:8, :].rearrange("p k h -> p (k h)"),
                        wexp_bf[n][:, 4096:8192], waits=w)
            ev_ew[n] = (b, [e1, e2])

        load_ew(0)
        load_ew(1)
        ev_ytl = None
        for n in range(NEXP_LOC):
            b, wev = ev_ew[n]
            for dg in range(2):
                j, bank, free = pool.get()
                for k in range(DC):
                    w = wev + [ev_xpt] + ([free] if k == 0 else [])
                    MM(bank[0:32, :], XPT[:, k, n * 32:(n + 1) * 32],
                       EXPWB[b][:, k, dg * 512:(dg + 1) * 512],
                       start=(k == 0), stop=False, waits=w)
                MM(bank[0:32, :], SEL8[0:8, n * 32:(n + 1) * 32],
                   EXPB[0:8, dg * 512:(dg + 1) * 512], start=False,
                   stop=True, waits=[ev_sel8], inc=kb.bump("T"))
                evt = kb.now("T")
                ev_ytl = kb.bump("A")
                ACT(YTL[0:32, n, dg * 512:(dg + 1) * 512], bank[0:32, :],
                    AF.Copy, waits=[evt], inc=ev_ytl)
                pool.done(j, ev_ytl)
            ev_ewfree[b] = kb.now("T")
            if n + 2 < NEXP_LOC:
                load_ew(n + 2)
        ev_agin = kb.dma("sync", ag_in.rearrange("(n p) h -> p n h", p=32),
                         YTL[:], waits=[ev_ytl])
        ev_cc2 = kb.bump("CC")

        def cc2(e):
            return e.collective_compute(
                "AllGather", ALU.bypass, replica_groups=[list(range(NCORES))],
                ins=[ag_in[:]], outs=[ag_out[:]])
        kb.emit("gpsimd", cc2, waits=[ev_agin], inc=ev_cc2)
        ev_yts = kb.dma("sync", YTS[:],
                        ag_out.rearrange("(s p) h -> p s h", p=128),
                        waits=[ev_cc2])
        ev_ew_done = kb.now("T")
        ev_rel = [None] * DC
        for dp in range(DC):
            w = [ev_ew_done] if dp < 4 else [ev_csm_c[2 * dp + 1]]
            ev_rel[dp] = kb.dma("scalar", XRES[:, dp, :],
                                xspill[:, dp * T:(dp + 1) * T], waits=w)
        res_evs = [None] * DC
        for g in range(TG):
            need = ev_csm_c[4 * g + 3]
            for dp in range(DC):
                j, bank, free = pool.get()
                for sc in range(16):
                    w = [ev_yts, need] + ([free] if sc == 0 else [])
                    MM(bank[:], YTS[:, sc, dp * 128:(dp + 1) * 128],
                       CSM[:, sc, tg(g)], start=(sc == 0), stop=(sc == 15),
                       waits=w, inc=(kb.bump("T") if sc == 15 else None))
                evt = kb.now("T")
                ev_res = kb.bump("V")
                V("tensor_tensor", out=XRES[:, dp, tg(g)],
                  in0=XRES[:, dp, tg(g)], in1=bank[:], op=ALU.add,
                  waits=[evt, ev_rel[dp], res_evs[dp]], inc=ev_res)
                pool.done(j, ev_res)
                res_evs[dp] = ev_res
        return res_evs

    ABS = SMALL
    XSYO = alloc("XSYO", [128, 8, T], BF16, offset=ABS)
    ZB = alloc("ZB", [128, 8, T], BF16, offset=ABS + 32768)
    XZ = alloc("XZ", [128, T], F32, offset=ABS + 65536)
    CV = alloc("CV", [128, T], F32, offset=ABS + 73728)
    WBB = alloc("WBB", [128, 16, 128], BF16, offset=ABS + 81920)
    BRP = alloc("BRP", [128, 4, T], BF16, offset=ABS + 32768)
    CRP = alloc("CRP", [128, 4, T], BF16, offset=ABS + 49152)
    MDB = [alloc("MDB0", [128, T], F32, offset=ABS + 65536),
           alloc("MDB1", [128, T], F32, offset=ABS + 73728)]
    XDBW = alloc("XDBW", [96, T], BF16, offset=65536)
    ABF = [alloc("AB0", [128, T], F32, offset=65536 + 4096),
           alloc("AB1", [128, T], F32, offset=65536 + 12288)]
    BB2 = [alloc("BB0", [128, T], BF16, offset=65536 + 20480),
           alloc("BB1", [128, T], BF16, offset=65536 + 24576)]
    DU = alloc("DU", [128, 8, T], BF16, offset=0)
    HBB = alloc("HBB2", [128, T], BF16, offset=32768)
    PB2 = [alloc("PB0", [128, T], BF16, offset=36864),
           alloc("PB1", [128, T], BF16, offset=40960)]
    SGB = XZ
    MDP = CV

    def attn_b(ev_ln, ev_scr, x_ev):
        x_evs = x_ev if isinstance(x_ev, list) else [x_ev] * DC
        ev_wb = {}
        ev_wbfree = {}

        def load_wbin(i):
            b = i % 2
            ev = kb.dma("gpsimd",
                        WBB[:, 0:8, :].rearrange("p k j -> p (k j)")
                        if b == 0 else WBB[:, 8:16, :].rearrange("p k j -> p (k j)"),
                        wbin[i], waits=[ev_wbfree.get(i)])
            ev_wb[i] = (b, ev)

        ev_wbfree[0] = ev_scr
        ev_wbfree[1] = ev_scr
        load_wbin(0)
        load_wbin(1)
        ev_xs = [None] * 8
        ev_zb = [None] * 8
        ev_conv_prev = None
        ev_xz_evac_prev = None
        for i in range(16):
            b, wev = ev_wb[i]
            wslab = WBB[:, 0:8, :] if b == 0 else WBB[:, 8:16, :]
            ev_ev = None
            for g in range(TG):
                j, bank, free = pool.get()
                for k in range(DC):
                    w = [wev, ev_ln] + ([free] if k == 0 else [])
                    MM(bank[:], wslab[:, k, :], LNX[:, k, tg(g)],
                       start=(k == 0), stop=(k == DC - 1), waits=w,
                       inc=(kb.bump("T") if k == DC - 1 else None))
                evt = kb.now("T")
                ev_ev = kb.bump("A")
                ACT(XZ[:, tg(g)], bank[:], AF.Copy,
                    waits=[evt, ev_conv_prev], inc=ev_ev)
                pool.done(j, ev_ev)
            ev_wbfree[i + 2] = kb.now("T")
            if i + 2 < 16:
                load_wbin(i + 2)
            kt = KXT if i < 8 else KZT
            kc = i if i < 8 else i - 8
            e1 = kb.bump("V")
            V("tensor_scalar_mul", CV[:, :], XZ[:, :], kt[:, kc * 4 + 1:kc * 4 + 2],
              waits=[ev_ev, ev_conv_prev], inc=e1)
            e2 = kb.bump("V")
            V("scalar_tensor_tensor", out=CV[:, 1:T], in0=XZ[:, 0:T - 1],
              scalar=kt[:, kc * 4 + 0:kc * 4 + 1], in1=CV[:, 1:T],
              op0=ALU.mult, op1=ALU.add, waits=[e1], inc=e2)
            e3 = kb.bump("V")
            V("scalar_tensor_tensor", out=CV[:, 0:T - 1], in0=XZ[:, 1:T],
              scalar=kt[:, kc * 4 + 2:kc * 4 + 3], in1=CV[:, 0:T - 1],
              op0=ALU.mult, op1=ALU.add, waits=[e2], inc=e3)
            e4 = kb.bump("V")
            V("scalar_tensor_tensor", out=CV[:, 0:T - 2], in0=XZ[:, 2:T],
              scalar=kt[:, kc * 4 + 3:kc * 4 + 4], in1=CV[:, 0:T - 2],
              op0=ALU.mult, op1=ALU.add, waits=[e3], inc=e4)
            es = kb.bump("A")
            dst = XSYO[:, i, :] if i < 8 else ZB[:, i - 8, :]
            ACT(dst, CV[:, :], AF.Silu, waits=[e4], inc=es)
            ev_conv_prev = es
            if i < 8:
                ev_xs[i] = es
            else:
                ev_zb[i - 8] = es

        ev_wo = {}
        ev_wofree = {0: kb.now("T"), 1: kb.now("T")}

        def load_wbo(dp, half_only=None):
            b = dp % 2
            sl = WBB[:, 0:8, :] if b == 0 else WBB[:, 8:16, :]
            src = wbo[dp].rearrange("p (i j) -> p i j", i=16)
            src = src[:, 8:16, :] if half_only == "z" else src[:, 0:8, :]
            ev = kb.dma("gpsimd", sl.rearrange("p k j -> p (k j)"),
                        src.rearrange("p i j -> p (i j)"),
                        waits=[ev_wofree.get(dp)])
            ev_wo[dp] = (b, ev)

        ev_wofree[0] = kb.now("T")
        ev_wofree[1] = kb.now("T")
        load_wbo(0, "z")
        load_wbo(1, "z")
        ev_zres = None
        for dp in range(DC):
            b, wev = ev_wo[dp]
            wslab = WBB[:, 0:8, :] if b == 0 else WBB[:, 8:16, :]
            for g in range(TG):
                j, bank, free = pool.get()
                for ii in range(8):
                    w = [wev, ev_zb[ii]] + ([free] if ii == 0 else [])
                    MM(bank[:], wslab[:, ii, :], ZB[:, ii, tg(g)],
                       start=(ii == 0), stop=(ii == 7), waits=w,
                       inc=(kb.bump("T") if ii == 7 else None))
                evt = kb.now("T")
                ev_zres = kb.bump("V")
                V("scalar_tensor_tensor", out=XRES[:, dp, tg(g)], in0=bank[:],
                  scalar=BOUTB[:, dp:dp + 1], in1=XRES[:, dp, tg(g)],
                  op0=ALU.add, op1=ALU.add, waits=[evt, x_evs[dp]], inc=ev_zres)
                pool.done(j, ev_zres)
            ev_wofree[dp + 2] = kb.now("T")
            if dp + 2 < DC:
                load_wbo(dp + 2, "z")

        ev_spill = kb.dma("sync", xspill, XRES[:].rearrange("p k t -> p (k t)"),
                          waits=[ev_zres])

        ev_xdb = None
        for g in range(TG):
            j, bank, free = pool.get()
            for k in range(DC):
                w = [ev_xs[k]] + ([free] if k == 0 else [])
                MM(bank[0:96, :], WXPB[:, k * 96:(k + 1) * 96],
                   XSYO[:, k, tg(g)], start=(k == 0), stop=(k == DC - 1),
                   waits=w, inc=(kb.bump("T") if k == DC - 1 else None))
            evt = kb.now("T")
            ev_xdb = kb.bump("A")
            ACT(XDBW[0:96, tg(g)], bank[0:96, :], AF.Copy,
                waits=[evt], inc=ev_xdb)
            pool.done(j, ev_xdb)

        ev_md_sp = [None] * 8
        ev_du = [None] * 8
        ev_yo = [None] * 8
        ev_sg_prev = ev_conv_prev
        ev_mdp_prev = ev_conv_prev
        ev_mdsp_prev = None
        for k in range(DC):
            banks = []
            for g in range(TG):
                j, bank, free = pool.get()
                MM(bank[:], WDT[0:64, k * 128:(k + 1) * 128],
                   XDBW[0:64, tg(g)], start=True, stop=True,
                   waits=[ev_xdb] + ([free] if True else []),
                   inc=kb.bump("T"))
                banks.append((j, bank, kb.now("T")))
            ev_sg = None
            for g, (j, bank, evt) in enumerate(banks):
                ev_sg = kb.bump("A")
                ACT(SGB[:, tg(g)], bank[:], AF.Sigmoid, scale=-1.0,
                    bias=NBDT[:, k:k + 1],
                    waits=[evt, ev_nbdt, ev_sg_prev], inc=ev_sg)
                pool.done(j, ev_sg)
            ev_mdp = kb.bump("A")
            ACT(MDP[:, :], SGB[:, :], AF.Ln,
                waits=[ev_sg, ev_mdp_prev, ev_mdsp_prev], inc=ev_mdp)
            ev_sg_prev = ev_mdp
            ev_md_sp[k] = kb.dma("sync", mdspill[:, k * T:(k + 1) * T],
                                 MDP[:, :], waits=[ev_mdp])
            ev_mdsp_prev = ev_md_sp[k]
            ev_du[k] = kb.bump("V")
            V("tensor_tensor", out=DU[:, k, :], in0=MDP[:, :],
              in1=XSYO[:, k, :], op=ALU.mult,
              waits=[ev_mdp, ev_xs[k], ev_spill], inc=ev_du[k])
            ev_mdp_prev = ev_du[k]
            ev_yo[k] = kb.bump("V")
            V("tensor_scalar_mul", XSYO[:, k, :], XSYO[:, k, :],
              DPAR[:, k:k + 1], waits=[ev_du[k]], inc=ev_yo[k])

        ev_mdb_init = [kb.now("A"), kb.now("V")]
        ev_mdb_q = kb.qdone("sync")
        ev_brp = [None] * 4
        ev_crp = [None] * 4
        ev_brp_cons = [None] * 4
        ev_crp_cons = [None] * 4
        ev_mdb_cons = ev_mdb_init
        ev_ab_cons = [None, None]
        ev_bb_cons = [None, None]
        ev_pb_cons = [None, None]
        ev_hb_cons = None
        ev_yo_prev = list(ev_yo)
        for grp in range(4):
            for m in range(4):
                n = grp * 4 + m
                for rep, (slab, dstb, evl) in enumerate(
                        ((SELBB, BRP, ev_brp), (SELCB, CRP, ev_crp))):
                    evv = None
                    for g in range(TG):
                        j, bank, free = pool.get()
                        MM(bank[:], slab[64:96, n * 128:(n + 1) * 128],
                           XDBW[64:96, tg(g)], start=True, stop=True,
                           waits=[ev_xdb, free,
                                  (ev_brp_cons if rep == 0 else ev_crp_cons)[m]],
                           inc=kb.bump("T"))
                        evt = kb.now("T")
                        evv = kb.bump("A")
                        ACT(dstb[:, m, tg(g)], bank[:], AF.Copy,
                            waits=[evt], inc=evv)
                        pool.done(j, evv)
                    evl[m] = evv
            for k in range(DC):
                mb = k % 2
                ev_mdl = kb.dma("sync", MDB[mb][:, :],
                                mdspill[:, k * T:(k + 1) * T],
                                waits=[ev_md_sp[k], ev_mdb_cons[mb]] + ev_mdb_q)
                ybanks = []
                for m in range(4):
                    n = grp * 4 + m
                    ab = m % 2
                    ea = kb.bump("A")
                    ACT(ABF[ab][:, :], MDB[mb][:, :], AF.Exp,
                        scale=APOS[:, k * 16 + n:k * 16 + n + 1],
                        waits=[ev_mdl, ev_aposx, ev_ab_cons[ab]], inc=ea)
                    eb = kb.bump("V")
                    V("tensor_tensor", out=BB2[ab][:, :], in0=DU[:, k, :],
                      in1=BRP[:, m, :], op=ALU.mult,
                      waits=[ev_du[k], ev_brp[m], ev_bb_cons[ab]], inc=eb)
                    if k == DC - 1:
                        ev_brp_cons[m] = eb
                    eh = kb.bump("V")
                    V("tensor_tensor_scan", out=HBB[:, :], data0=ABF[ab][:, :],
                      data1=BB2[ab][:, :], initial=0.0, op0=ALU.mult,
                      op1=ALU.add, waits=[ea, eb, ev_hb_cons], inc=eh)
                    ev_ab_cons[ab] = eh
                    ev_bb_cons[ab] = eh
                    ep = kb.bump("V")
                    V("tensor_tensor", out=PB2[ab][:, :], in0=HBB[:, :],
                      in1=CRP[:, m, :], op=ALU.mult,
                      waits=[eh, ev_crp[m], ev_pb_cons[ab]], inc=ep)
                    if k == DC - 1:
                        ev_crp_cons[m] = ep
                    ev_hb_cons = ep
                    if m == 0:
                        for g in range(TG):
                            j, bank, free = pool.get()
                            ybanks.append((j, bank))
                            MM(bank[:], NIDENT[:], PB2[ab][:, tg(g)],
                               start=True, stop=False, waits=[ep, free],
                               inc=(kb.bump("T") if g == TG - 1 else None))
                    else:
                        for g in range(TG):
                            j, bank = ybanks[g]
                            MM(bank[:], NIDENT[:], PB2[ab][:, tg(g)],
                               start=False, stop=(m == 3), waits=[ep],
                               inc=(kb.bump("T") if g == TG - 1 else None))
                    ev_pb_cons[ab] = kb.now("T")
                ev_mdb_cons[mb] = kb.now("A")
                evt = kb.now("T")
                for g in range(TG):
                    j, bank = ybanks[g]
                    ey = kb.bump("V")
                    V("tensor_tensor", out=XSYO[:, k, tg(g)],
                      in0=XSYO[:, k, tg(g)], in1=bank[:], op=ALU.add,
                      waits=[evt, ev_yo_prev[k]], inc=ey)
                    pool.done(j, ey)
                    ev_yo_prev[k] = ey


        ev_xrel = kb.dma("sync", XRES[:].rearrange("p k t -> p (k t)"), xspill,
                         waits=[kb.now("V"), kb.now("T")])
        ev_wo2 = {}
        ev_wo2free = {0: kb.now("T"), 1: kb.now("T")}

        def load_wbo2(dp):
            b = dp % 2
            sl = WBB[:, 0:8, :] if b == 0 else WBB[:, 8:16, :]
            src = wbo[dp].rearrange("p (i j) -> p i j", i=16)[:, 0:8, :]
            ev = kb.dma("gpsimd", sl.rearrange("p k j -> p (k j)"),
                        src.rearrange("p i j -> p (i j)"),
                        waits=[ev_wo2free.get(dp)])
            ev_wo2[dp] = (b, ev)

        load_wbo2(0)
        load_wbo2(1)
        res_evs = [None] * DC
        for dp in range(DC):
            b, wev = ev_wo2[dp]
            wslab = WBB[:, 0:8, :] if b == 0 else WBB[:, 8:16, :]
            for g in range(TG):
                j, bank, free = pool.get()
                for ii in range(8):
                    w = [wev, ev_yo_prev[ii]] + ([free] if ii == 0 else [])
                    MM(bank[:], wslab[:, ii, :], XSYO[:, ii, tg(g)],
                       start=(ii == 0), stop=(ii == 7), waits=w,
                       inc=(kb.bump("T") if ii == 7 else None))
                evt = kb.now("T")
                ev_res = kb.bump("V")
                V("tensor_tensor", out=XRES[:, dp, tg(g)],
                  in0=XRES[:, dp, tg(g)], in1=bank[:], op=ALU.add,
                  waits=[evt, ev_xrel], inc=ev_res)
                pool.done(j, ev_res)
            res_evs[dp] = ev_res
            ev_wo2free[dp + 2] = kb.now("T")
            if dp + 2 < DC:
                load_wbo2(dp + 2)
        return res_evs

    x_ev = ev_x
    if PHASES >= 1:
        ev_ln1, ev_scr = layernorm(x_ev)
        if PHASES >= 2:
            x_ev = attn_a(ev_ln1, ev_scr)
    ev_phibf = kb.dma("gpsimd", phi_bf[:], phi[:])
    ev_ewbf = [kb.dma("gpsimd", wexp_bf[n], expw[n]) for n in range(NEXP_LOC)]
    if PHASES >= 3:
        ev_ln2, ev_scr2 = layernorm(x_ev)
        if PHASES >= 4:
            x_ev = ffn(0, ev_ln2, ev_scr2, x_ev)
    if PHASES >= 5:
        ev_ln3, ev_scr3 = layernorm(x_ev, guard=kb.now("T"))
        if PHASES >= 6:
            x_ev = attn_b(ev_ln3, ev_scr3, x_ev)
    if PHASES >= 7:
        ev_ln4, ev_scr4 = layernorm(x_ev, guard=kb.now("T"))
        if PHASES >= 8:
            x_ev = ffn(1, ev_ln4, ev_scr4, x_ev)

    if DEBUG_OUT == "lnx":
        ev_store = kb.dma("gpsimd", y_out.rearrange("(k p) t -> p k t", p=128),
                          LNX[:], waits=[kb.now("V"), kb.now("A"), kb.now("T")])
    else:
        xw = x_ev if isinstance(x_ev, list) else [x_ev]
        ev_store = kb.dma("sync", y_out.rearrange("(k p) t -> p k t", p=128),
                          XRES[:], waits=list(xw))
    kb.emit("sync", lambda e: e.nop(), waits=[ev_store])

    with nc.Block() as block:
        for eng in ENGS:
            fns = kb.q[eng]
            getattr(block, eng)(lambda e, fns=fns: [f(e) for f in fns])
    return kb



def _prep_core(c, I):
    pk = np.ascontiguousarray
    d = {}
    d["x"] = pk(I["L0_attn_a_x"][c].T)

    def packW(Wt, ec, kc):
        out = np.empty((ec, 128, kc * 128), np.float32)
        for e in range(ec):
            sl = Wt[:, e * 128:(e + 1) * 128]
            out[e] = sl.reshape(kc, 128, 128).transpose(1, 0, 2).reshape(128, kc * 128)
        return pk(out)

    d["wa0"] = packW(I["L0_attn_a_W_delta"].T, 8, 8)
    d["wa1"] = packW(I["L0_attn_a_W_select"].T, 8, 8)
    d["wa2"] = packW(I["L0_attn_a_W_in"].T, 8, 8)
    d["wa3"] = packW(I["L0_attn_a_W_gate"].T, 8, 8)
    d["wao"] = packW(I["L0_attn_a_W_out"].T, 8, 8)
    d["a_log"] = pk(I["L0_attn_a_log_a"].reshape(8, 128).T)

    d["phi"] = pk(I["L0_ffn_phi_w"].T.reshape(8, 128, 2048))
    d["phib"] = pk(I["L0_ffn_phi_b"].reshape(1, 2048))
    ew = I["L0_ffn_expert_w"][c * 8:(c + 1) * 8]
    d["expw"] = pk(ew.reshape(8, 8, 128, 1024).transpose(0, 2, 1, 3).reshape(8, 128, 8192))
    d["expb"] = pk(I["L0_ffn_expert_b"][c * 8:(c + 1) * 8])

    d["wbin"] = packW(I["L0_attn_b_W_in"], 16, 8)
    d["kx"] = pk(I["L0_attn_b_K_x"][:, 0, :].reshape(4, 8, 128).transpose(2, 1, 0).reshape(128, 32))
    d["kz"] = pk(I["L0_attn_b_K_z"][:, 0, :].reshape(4, 8, 128).transpose(2, 1, 0).reshape(128, 32))
    d["wxp"] = pk(I["L0_attn_b_W_xp"].reshape(8, 128, 96).transpose(1, 0, 2).reshape(128, 768))
    d["wdt"] = pk(I["L0_attn_b_W_dt"])
    d["b_dt"] = pk(I["L0_attn_b_b_dt"].reshape(8, 128).T)
    d["a_logb"] = pk(I["L0_attn_b_A_log"].reshape(8, 128, 16).transpose(1, 0, 2).reshape(128, 128))
    d["d_par"] = pk(I["L0_attn_b_D_param"].reshape(8, 128).T)
    d["b_outb"] = pk(I["L0_attn_b_b_out"].reshape(8, 128).T)
    d["wbo"] = packW(I["L0_attn_b_W_out"], 8, 16)

    d["identm"] = np.eye(128, dtype=np.float32)
    d["nidentm"] = -np.eye(128, dtype=np.float32)
    d["ones1"] = np.ones((1, 128), np.float32)
    d["onescol"] = np.full((128, 1), 1.0 / 1024.0, np.float32)
    d["ones1n"] = np.full((1, 128), -1.0, np.float32)
    selb = np.zeros((128, 2048), np.float32)
    selc = np.zeros((128, 2048), np.float32)
    for n in range(16):
        selb[64 + n, n * 128:(n + 1) * 128] = 1.0
        selc[80 + n, n * 128:(n + 1) * 128] = 1.0
    d["selb"] = selb
    d["selc"] = selc
    s8 = np.zeros((8, 256), np.float32)
    for n in range(8):
        s8[n, n * 32:(n + 1) * 32] = 1.0
    d["sel8"] = s8
    return d


def kernel(**inputs):
    kb = build_program()
    in_maps = [_prep_core(c, inputs) for c in range(NCORES)]
    res = run_bass_kernel_spmd(kb.nc, in_maps, core_ids=list(range(NCORES)))
    kb.stack.close()
    return np.stack([np.ascontiguousarray(res.results[c]["y"].T)
                     for c in range(NCORES)])
